# version 28
# baseline (speedup 1.0000x reference)
import numpy as np
import ml_dtypes
from contextlib import ExitStack

import concourse.bass as bass
import concourse.mybir as mybir
from concourse import dve_ops as _dve_ops
from concourse.bass_utils import run_bass_kernel_spmd

BF = ml_dtypes.bfloat16
F32 = mybir.dt.float32
BF16 = mybir.dt.bfloat16
AF = mybir.ActivationFunctionType
ALU = mybir.AluOpType
AX = mybir.AxisListType

NCORES = 8
NSEQ = 2
S = 2048
D = 1024
INNER = 4096
NT = S // 128
NW = S // 512
EPS = 1e-5

ENGS = ("sync", "tensor", "scalar", "vector", "gpsimd")
_DEBUG = False


class Sched:
    def __init__(self, nc):
        self.nc = nc
        self.sems = {}
        self.cnt = {}
        self.items = {e: [] for e in ENGS}
        self.hwm = {e: {} for e in ENGS}
        self.dma_streams = {}

    def sem(self, name):
        if name not in self.sems:
            self.sems[name] = self.nc.alloc_semaphore(name)
            self.cnt[name] = 0
        return self.sems[name]

    def op(self, eng, fn, waits=(), inc=None, dk=1):
        waits = list(waits)
        if inc in ("dve", "act", "gp"):
            self.sem(inc)
            waits.append((inc, self.cnt[inc]))
        merged = {}
        for wv in waits:
            if wv is None:
                continue
            s, v = wv
            if v is None or v <= 0:
                continue
            if v > merged.get(s, 0):
                merged[s] = v
        ws = []
        for s, v in merged.items():
            self.sem(s)
            if v > self.hwm[eng].get(s, 0):
                ws.append((s, v))
                self.hwm[eng][s] = v
        mark = None
        if inc is not None:
            self.sem(inc)
            self.cnt[inc] += dk
            mark = (inc, self.cnt[inc])
        self.items[eng].append((ws, fn, inc, dk))
        return mark

    def dma(self, eng, out_ap, in_ap, waits=(), stream="misc"):
        sname = "D_" + stream
        self.dma_streams.setdefault(eng, set()).add(sname)
        return self.op(eng, lambda e, o=out_ap, i=in_ap: e.dma_start(out=o, in_=i),
                       waits=waits, inc=sname, dk=16)

    def snap(self, name):
        self.sem(name)
        return (name, self.cnt[name])

    def barrier(self):
        for e in ENGS:
            w = [(s, self.cnt[s]) for s in self.dma_streams.get(e, ())]
            self.op(e, lambda eng: eng.nop(), waits=w, inc="bar")
        tgt = self.cnt["bar"]
        for e in ENGS:
            self.op(e, None, waits=[("bar", tgt)])
        return ("bar", tgt)

    def build(self, block):
        for ename in ENGS:
            items = self.items[ename]

            def body(eng, items=items):
                for ws, fn, inc, dk in items:
                    for s, v in ws:
                        eng.wait_ge(self.sems[s], v)
                    if fn is None:
                        continue
                    inst = fn(eng)
                    if inc is not None and inst is not None:
                        inst.then_inc(self.sems[inc], dk)

            getattr(block, ename)(body)


def build_program():
    nc = bass.Bass("TRN2", target_bir_lowering=False)

    x_in = nc.dram_tensor("x", [NSEQ, S, D], F32, kind="ExternalInput").ap()
    wqkv = nc.dram_tensor("wqkv", [D, 3 * D], F32, kind="ExternalInput").ap()
    bqkv = nc.dram_tensor("bqkv", [3 * D], F32, kind="ExternalInput").ap()
    wout = nc.dram_tensor("wout", [D, D], F32, kind="ExternalInput").ap()
    bout = nc.dram_tensor("bout", [D], F32, kind="ExternalInput").ap()
    wgT = nc.dram_tensor("wgT", [D, INNER], F32, kind="ExternalInput").ap()
    wuT = nc.dram_tensor("wuT", [D, INNER], F32, kind="ExternalInput").ap()
    wdT = nc.dram_tensor("wdT", [INNER, D], F32, kind="ExternalInput").ap()
    out_hbm = nc.dram_tensor("out", [NSEQ, S, D], F32, kind="ExternalOutput").ap()
    dbg = {}
    if _DEBUG:
        for nm, shp, dt in [("d_KT", [128, 8 * S], "bf"), ("d_VT", [128, NT * 1024], "bf"),
                            ("d_QTw", [128, 8 * 512], "bf"), ("d_valsT", [128, 8 * 512], "bf"),
                            ("d_x1", [128, NT * 1024], "f"), ("d_h2qT", [128, 8 * 512], "bf"),
                            ("d_y", [128, 4 * 4096], "bf"), ("d_osb", [128, 4 * 1024], "f"),
                            ("d_wq", [128, 4096], "bf"), ("d_sq", [128, NT], "f"),
                            ("d_ffa", [128, NT], "f"), ("d_ffb", [128, NT], "f"),
                            ("d_ffc1", [128, NT], "f"), ("d_ffmx", [128, NT], "f"),
                            ("d_ffr", [128, NT], "f"), ("d_ffm", [128, NT], "f"),
                            ("d_sta", [128, NT], "f"), ("d_recb", [1, 1024], "bf")]:
            dbg[nm] = nc.dram_tensor(nm, shp, BF16 if dt == "bf" else F32,
                                     kind="ExternalOutput").ap()
    wqkv_bf = nc.dram_tensor("wqkv_bf", [D, 3 * D], BF16, kind="Internal").ap()
    wgq = nc.dram_tensor("wgq", [D, INNER], BF16, kind="Internal").ap()
    wuq = nc.dram_tensor("wuq", [D, INNER], BF16, kind="Internal").ap()
    wdq = nc.dram_tensor("wdq", [INNER, D], BF16, kind="Internal").ap()

    ident_bf_h = nc.inline_tensor(np.eye(128, dtype=BF), name="c_ident_bf")
    ident_f_h = nc.inline_tensor(np.eye(128, dtype=np.float32), name="c_ident_f")
    tri_np = np.where(np.arange(128)[:, None] > np.arange(128)[None, :],
                      np.float32(-30000.0), np.float32(0.0)).astype(BF)
    tri_h = nc.inline_tensor(tri_np, name="c_tri")

    sch = Sched(nc)
    es_root = ExitStack()
    sb = lambda n, s_, d=F32: es_root.enter_context(nc.sbuf_tensor(n, s_, d))

    ident_bf = sb("ident_bf", [128, 128], BF16)
    tri_bf = sb("tri_bf", [128, 128], BF16)
    ones_col = sb("ones_col", [128, 1], BF16)
    onesr_bf = sb("onesr_bf", [1, 128], BF16)
    onesr_f = sb("onesr_f", [1, 128], F32)
    cst = sb("cst", [128, 8], F32)
    bqkvT = sb("bqkvT", [128, 24], F32)
    beff = sb("beff", [128, 1024], BF16)
    sqj_g = sb("sqj_g", [128, 1024], BF16)
    mcb = sb("mcb", [128, 4], F32)
    thrb = sb("thrb", [128, 4], F32)
    wsums = sb("wsums", [128, 48], F32)
    wtot = sb("wtot", [1, 8], F32)
    st_sx = sb("st_sx", [128, NT], F32)
    st_sq = sb("st_sq", [128, NT], F32)
    st_mx = sb("st_mx", [128, NT], F32)
    st_mn = sb("st_mn", [128, NT], F32)
    st_m = sb("st_m", [128, NT], F32)
    st_r = sb("st_r", [128, NT], F32)
    st_a = sb("st_a", [128, NT], F32)
    st_b = sb("st_b", [128, NT], F32)
    st_c1 = sb("st_c1", [128, NT], F32)
    st_c2 = sb("st_c2", [128, NT], F32)
    st_my = sb("st_my", [128, NT], F32)
    st_p = sb("st_p", [128, 32], F32)
    st_cd = sb("st_cd", [128, NT], F32)
    st_w = sb("st_w", [128, 16], F32)
    st_w2 = sb("st_w2", [128, 16], F32)
    st_wa = sb("st_wa", [128, 16], F32)
    st_wb = sb("st_wb", [128, 16], F32)
    st_wm = sb("st_wm", [128, 16], F32)
    st_wr = sb("st_wr", [128, 16], F32)
    x_sb = sb("x_sb", [128, NT * 1024], F32)

    def tap(name, sbuf_ap, waits=()):
        if _DEBUG and name in dbg:
            return sch.dma("sync", dbg[name], sbuf_ap, waits=waits, stream="tap")
        return None

    def vop(fn, waits=(), inc="dve"):
        return sch.op("vector", fn, waits=waits, inc=inc)

    def aop(fn, waits=(), inc="act"):
        return sch.op("scalar", fn, waits=waits, inc=inc)

    def top(fn, waits=(), inc="pe"):
        return sch.op("tensor", fn, waits=waits, inc=inc)

    def ln_chain(sx, sq, m_t, r_t, a_t, b_t, s_, denom=1024.0):
        act_now = sch.snap("act")
        m1 = vop(lambda e: e.tensor_single_scalar(
            out=m_t[:, s_], in_=sx[:, s_], scalar=1.0 / denom, op=ALU.mult),
            waits=[act_now])
        vop(lambda e: e.tensor_tensor(
            out=b_t[:, s_], in0=m_t[:, s_], in1=m_t[:, s_], op=ALU.mult))
        m3a = vop(lambda e: e.scalar_tensor_tensor(
            out=r_t[:, s_], in0=sq[:, s_], scalar=1.0 / denom, in1=b_t[:, s_],
            op0=ALU.mult, op1=ALU.subtract))
        m3 = vop(lambda e: e.tensor_single_scalar(
            out=r_t[:, s_], in_=r_t[:, s_], scalar=0.0, op=ALU.max))
        a1 = aop(lambda e: e.activation(
            out=r_t[:, s_], in_=r_t[:, s_], func=AF.Ln, bias=cst[:, 2:3], scale=1.0), waits=[m3])
        a2 = aop(lambda e: e.activation(
            out=r_t[:, s_], in_=r_t[:, s_], func=AF.Exp, bias=0.0, scale=-0.5), waits=[a1])
        m4 = vop(lambda e: e.tensor_copy(out=a_t[:, s_], in_=r_t[:, s_]), waits=[a2])
        m5 = vop(lambda e: e.scalar_tensor_tensor(
            out=b_t[:, s_], in0=m_t[:, s_], scalar=-1.0, in1=a_t[:, s_],
            op0=ALU.mult, op1=ALU.mult), waits=[m4])
        return m5

    es_prep_sb = ExitStack()
    ident_f = es_prep_sb.enter_context(nc.sbuf_tensor("ident_f", [128, 128], F32))
    b24 = es_prep_sb.enter_context(nc.sbuf_tensor("b24", [24, 128], F32))
    sch.dma("sync", ident_bf[:], ident_bf_h.ap(), stream="cst")
    sch.dma("sync", ident_f[:], ident_f_h.ap(), stream="cst")
    sch.dma("sync", tri_bf[:], tri_h.ap(), stream="cst")
    sch.dma("sync", b24[:], bqkv.rearrange("(c p) -> c p", p=128), stream="cst")
    ld0 = sch.snap("D_cst")

    vop(lambda e: e.memset(ones_col[:], 1.0))
    vop(lambda e: e.memset(onesr_bf[:], 1.0))
    vop(lambda e: e.memset(onesr_f[:], 1.0))
    vop(lambda e: e.memset(cst[:, 1:2], 0.125))
    vop(lambda e: e.memset(cst[:, 2:3], EPS))
    vop(lambda e: e.memset(cst[:, 3:4], 1.0))
    vop(lambda e: e.memset(cst[:, 5:6], -1.0))
    vop(lambda e: e.memset(cst[:, 4:5], 1.0))
    vop(lambda e: e.tensor_scalar(out=cst[:, 4:5], in0=cst[:, 4:5], scalar1=2.0 ** -30,
                                  scalar2=1.0, op0=ALU.add, op1=ALU.subtract))
    m_magic = vop(lambda e: e.tensor_scalar(
        out=cst[:, 0:1], in0=cst[:, 4:5],
        scalar1=2.0 ** 30 * (1.5 * 2.0 ** 52 - 1.5 * 2.0 ** 23), scalar2=1.5 * 2.0 ** 23,
        op0=ALU.mult, op1=ALU.add))

    es_prep_ps = ExitStack()
    pbias = es_prep_ps.enter_context(nc.psum_tensor("pbias", [128, 24], F32))
    trm = top(lambda e: e.transpose(pbias[:], b24[:], ident_f[0:24, 0:24]), waits=[ld0])
    evm = vop(lambda e: e.tensor_copy(out=bqkvT[:], in_=pbias[:]), waits=[trm])
    bk_mark = vop(lambda e: e.tensor_single_scalar(
        out=bqkvT[:, 8:16], in_=bqkvT[:, 8:16], scalar=0.125, op=ALU.mult), waits=[evm])

    sch.barrier()

    prep = ExitStack()
    psb = lambda n, s_, d=F32: prep.enter_context(nc.sbuf_tensor(n, s_, d))
    wf = [psb(f"wf{i}", [128, 4096], F32) for i in range(3)]
    wjunk = psb("wjunk", [128, 4096], BF16)
    wabs = [psb(f"wabs{i}", [128, 4096], F32) for i in range(2)]
    wsgn = [psb(f"wsgn{i}", [128, 4096], BF16) for i in range(2)]
    wqb = [psb(f"wqb{i}", [128, 4096], BF16) for i in range(2)]
    onescol_f = psb("onescol_f", [128, 1], F32)
    vop(lambda e: e.memset(onescol_f[:], 1.0))

    for cb in range(6):
        sch.dma("gpsimd", wqkv_bf[:, cb * 512:(cb + 1) * 512],
                wqkv[:, cb * 512:(cb + 1) * 512], stream=f"wqc{cb}")

    mats = [(wgT, wgq, 8, 4096), (wuT, wuq, 8, 4096), (wdT, wdq, 32, 1024)]
    col = 0
    tmarks = []
    for mi, (wsrc, wdst, ntile, wid) in enumerate(mats):
        for t in range(ntile):
            buf = wf[col % 3]
            w = [tmarks[-3]] if len(tmarks) >= 3 else []
            ldm = sch.dma("sync", buf[:, 0:wid], wsrc[t * 128:(t + 1) * 128, :],
                          waits=w, stream=f"wf{col % 3}")
            am = aop(lambda e, b=buf, wd=wid, c=col: e.activation(
                out=wjunk[:, 0:wd], in_=b[:, 0:wd], func=AF.Abs,
                accum_out=wsums[:, c:c + 1]), waits=[ldm])
            tmarks.append(am)
            col += 1
    ncols = col

    ptot = es_prep_ps.enter_context(nc.psum_tensor("ptot", [1, 64], F32))
    pbc = es_prep_ps.enter_context(nc.psum_tensor("pbc", [128, 8], F32))
    tm = top(lambda e: e.matmul(ptot[0:1, 0:ncols], onescol_f[:], wsums[:, 0:ncols],
                                start=True, stop=True),
             waits=[tmarks[-1], sch.snap("dve")])
    c0 = 0
    redm = []
    for mi, (wsrc, wdst, ntile, wid) in enumerate(mats):
        rm = vop(lambda e, a=c0, b=c0 + ntile, i=mi: e.tensor_reduce(
            out=wtot[0:1, i:i + 1], in_=ptot[0:1, a:b], axis=AX.X, op=ALU.add), waits=[tm])
        redm.append(rm)
        c0 += ntile
    scm = []
    for mi, (wsrc, wdst, ntile, wid) in enumerate(mats):
        n_el = float(ntile * 128 * wid)
        scm.append(vop(lambda e, i=mi, ne=n_el: e.tensor_scalar(
            out=wtot[0:1, i:i + 1], in0=wtot[0:1, i:i + 1],
            scalar1=1.0 / ne, scalar2=1e-5, op0=ALU.mult, op1=ALU.max), waits=[redm[mi]]))
    thrm = vop(lambda e: e.tensor_single_scalar(
        out=wtot[0:1, 4:7], in_=wtot[0:1, 0:3], scalar=0.5, op=ALU.mult), waits=scm)
    bc1 = top(lambda e: e.matmul(pbc[:, 0:3], onesr_f[:], wtot[0:1, 0:3],
                                 start=True, stop=True), waits=[thrm])
    bc2 = top(lambda e: e.matmul(pbc[:, 4:7], onesr_f[:], wtot[0:1, 4:7],
                                 start=True, stop=True), waits=[thrm])
    vop(lambda e: e.tensor_copy(out=mcb[:, 0:3], in_=pbc[:, 0:3]), waits=[bc1, bc2])
    vop(lambda e: e.tensor_copy(out=thrb[:, 0:3], in_=pbc[:, 4:7]), waits=[bc1, bc2])
    sch.barrier()

    prevq = []
    last_qm = {}
    st_marks = []
    col = 0
    for mi, (wsrc, wdst, ntile, wid) in enumerate(mats):
        for t in range(ntile):
            buf = wf[col % 3]
            par = col % 2
            w = [prevq[-3]] if len(prevq) >= 3 else []
            ldm = sch.dma("sync", buf[:, 0:wid], wsrc[t * 128:(t + 1) * 128, :],
                          waits=w, stream=f"wf{col % 3}")
            a1 = aop(lambda e, b=buf, wd=wid, par=par: e.activation(
                out=wabs[par][:, 0:wd], in_=b[:, 0:wd], func=AF.Abs),
                waits=[ldm, last_qm.get(par)])
            a2 = aop(lambda e, b=buf, wd=wid, par=par: e.activation(
                out=wsgn[par][:, 0:wd], in_=b[:, 0:wd], func=AF.Sign), waits=[])
            qm = vop(lambda e, wd=wid, i=mi, par=par: e.scalar_tensor_tensor(
                out=wqb[par][:, 0:wd], in0=wabs[par][:, 0:wd], scalar=thrb[:, i:i + 1],
                in1=wsgn[par][:, 0:wd], op0=ALU.is_gt, op1=ALU.mult),
                waits=[a1, a2, st_marks[-2] if len(st_marks) >= 2 else None])
            last_qm[par] = qm
            stm = sch.dma("gpsimd", wdst[t * 128:(t + 1) * 128, :], wqb[par][:, 0:wid],
                          waits=[qm], stream=f"wst{par}")
            st_marks.append(stm)
            prevq.append(qm)
            if mi == 0 and t == 0:
                tpm = tap("d_wq", wqb[par][:], waits=[qm])
                if tpm:
                    st_marks.append(tpm)
            col += 1
    prevq = st_marks
    sch.barrier()
    prep.close()
    es_prep_sb.close()
    es_prep_ps.close()

    sp = [es_root.enter_context(nc.psum_tensor(f"sp{i}", [128, 1024], F32)) for i in range(2)]
    vps = [es_root.enter_context(nc.psum_tensor(f"vps{i}", [128, 512], F32)) for i in range(2)]
    sg = es_root.enter_context(nc.psum_tensor("sg", [128, 512], F32))
    tr_ps = es_root.enter_context(nc.psum_tensor("tr_ps", [128, 1024], BF16))
    pb = [sp[0][:, 0:512], sp[0][:, 512:1024], sp[1][:, 0:512], sp[1][:, 512:1024]]
    ps = pb + [vps[0][:], vps[1][:], sg[:]]

    def emit_seq(s_idx, xmarks=None):
        esA = ExitStack()
        ab = lambda n, s_, d=F32: esA.enter_context(nc.sbuf_tensor(f"s{s_idx}_{n}", s_, d))
        KT = ab("KT", [128, 8 * S], BF16)
        VT = ab("VT", [128, NT * 1024], BF16)
        QTw = ab("QTw", [128, 8 * 512], BF16)
        hln0 = ab("hln0", [128, 1024], BF16)
        hln = [hln0, hln0]
        hlnT = ab("hlnT", [128, 8 * 512], BF16)
        wsl = [ab(f"wsl{i}", [128, 8 * 128], BF16) for i in range(4)]
        wvs0 = ab("wvs0", [128, 8 * 512], BF16)
        wvs = [wvs0, wvs0]
        wo_sb = ab("wo_sb", [128, 8 * 1024], BF16)
        PT = [ab(f"PT{i}", [128, 1024], BF16) for i in range(2)]
        recf = ab("recf", [128, 512], F32)
        recb = ab("recb", [1, 1024], BF16)
        bcs = ab("bcs", [128, 512], F32)
        valsT = ab("valsT", [128, 8 * 512], BF16)
        attn = QTw
        sqj = sqj_g

        next_xmarks = {}
        if xmarks is None:
            mk_xt = {}
            for tt in range(NT):
                mk_xt[tt] = sch.dma("sync", x_sb[:, tt * 1024:(tt + 1) * 1024],
                                    x_in[s_idx, tt * 128:(tt + 1) * 128, :],
                                    waits=[], stream=f"x{tt}")
        else:
            mk_xt = xmarks
        mk_wo = sch.dma("gpsimd", wo_sb[:].rearrange("p (c j) -> p c j", c=8),
                        wout.rearrange("(c p) j -> p c j", p=128), stream="wo")

        if s_idx == 0:
            bvb = ab("bvb", [128, 8], BF16)
            lb0 = sch.dma("sync", recf[0:1, 0:512], bout[0:512].rearrange("(o j) -> o j", o=1), stream="cst")
            lb1 = sch.dma("sync", bcs[0:1, 0:512], bout[512:1024].rearrange("(o j) -> o j", o=1), stream="cst")
            m1 = vop(lambda e: e.tensor_copy(out=bvb[:], in_=bqkvT[:, 16:24]),
                     waits=[bk_mark])
            last = None
            for c in range(8):
                last = top(lambda e, c=c: e.matmul(
                    ps[0][0:1, :], bvb[:, c:c + 1], wo_sb[:, c * 1024:c * 1024 + 512],
                    start=(c == 0), stop=(c == 7)),
                    waits=[m1, mk_wo] if c == 0 else [])
            for c in range(8):
                last = top(lambda e, c=c: e.matmul(
                    ps[1][0:1, :], bvb[:, c:c + 1], wo_sb[:, c * 1024 + 512:(c + 1) * 1024],
                    start=(c == 0), stop=(c == 7)), waits=[])
            e1 = vop(lambda e: e.tensor_tensor(out=recf[0:1, :], in0=ps[0][0:1, :],
                                               in1=recf[0:1, :], op=ALU.add),
                     waits=[last, lb0, lb1])
            e2 = vop(lambda e: e.tensor_tensor(out=bcs[0:1, :], in0=ps[1][0:1, :],
                                               in1=bcs[0:1, :], op=ALU.add), waits=[])
            b1 = top(lambda e: e.matmul(ps[2][:], onesr_f[:], recf[0:1, :],
                                        start=True, stop=True), waits=[e1, e2])
            b2 = top(lambda e: e.matmul(ps[3][:], onesr_f[:], bcs[0:1, :],
                                        start=True, stop=True), waits=[])
            vop(lambda e: e.tensor_copy(out=beff[:, 0:512], in_=ps[2][:]), waits=[b1, b2])
            vop(lambda e: e.tensor_copy(out=beff[:, 512:1024], in_=ps[3][:]), waits=[])
            sch.barrier()

        ln1_marks = {}
        for wi in range(NW):
            for j in range(4):
                tt = wi * 4 + j
                xt = x_sb[:, tt * 1024:(tt + 1) * 1024]
                vop(lambda e, xt=xt, tt=tt: e.tensor_reduce(
                    out=st_sx[:, tt:tt + 1], in_=xt, axis=AX.X, op=ALU.add),
                    waits=[mk_xt[tt]])
                aop(lambda e, xt=xt, tt=tt: e.activation(
                    out=sqj[:], in_=xt, func=AF.Square, accum_out=st_sq[:, tt:tt + 1]),
                    waits=[mk_xt[tt]])
            ln1_marks[wi] = ln_chain(st_sx, st_sq, st_m, st_r, st_a, st_b,
                                     slice(wi * 4, wi * 4 + 4))

        x1_marks = {}
        vt_marks = {}
        kt_marks = {}
        ln2q_marks = {}
        state = {"tr_free": None, "hln_rd": [None]}
        bank_rd = [None] * 7
        slice_rd = [None, None, None, None]
        vslice_rd = [None, None]
        pt_free = [None, None]
        bcs_rd = None
        recb_rd = None

        def emit_hlnT(wi):
            last = None
            for j in range(4):
                tt = wi * 4 + j
                hb = hln[0]
                ap_m = vop(lambda e, tt=tt, hb=hb: e.tensor_scalar(
                    out=hb[:], in0=x_sb[:, tt * 1024:(tt + 1) * 1024],
                    scalar1=st_a[:, tt:tt + 1], scalar2=st_b[:, tt:tt + 1],
                    op0=ALU.mult, op1=ALU.add),
                    waits=[ln1_marks[wi], state["hln_rd"][0]])
                trs = None
                for c in range(8):
                    trs = top(lambda e, hb=hb, c=c: e.transpose(
                        tr_ps[:, c * 128:(c + 1) * 128], hb[:, c * 128:(c + 1) * 128],
                        ident_bf[:]), waits=[ap_m, state["tr_free"]] if c == 0 else [])
                state["hln_rd"][0] = trs
                last = vop(lambda e, j=j: e.tensor_copy(
                    out=hlnT[:].rearrange("p (c t) -> p c t", c=8)[:, :, j * 128:(j + 1) * 128],
                    in_=tr_ps[:].rearrange("p (c t) -> p c t", c=8)), waits=[trs])
                state["tr_free"] = last
            return last


        def emit_ln2_group(gi):
            for j in range(4):
                tt = gi * 4 + j
                xt = x_sb[:, tt * 1024:(tt + 1) * 1024]
                vop(lambda e, xt=xt, tt=tt: e.tensor_reduce(
                    out=st_sx[:, tt:tt + 1], in_=xt, axis=AX.X, op=ALU.add),
                    waits=[x1_marks[tt]])
                vop(lambda e, xt=xt, tt=tt: e.tensor_reduce(
                    out=st_mx[:, tt:tt + 1], in_=xt, axis=AX.X, op=ALU.max), waits=[])
                vop(lambda e, xt=xt, tt=tt: e.tensor_reduce(
                    out=st_mn[:, tt:tt + 1], in_=xt, axis=AX.X, op=ALU.min), waits=[])
                aop(lambda e, xt=xt, tt=tt: e.activation(
                    out=sqj_g[:], in_=xt, func=AF.Square, accum_out=st_sq[:, tt:tt + 1]),
                    waits=[x1_marks[tt]])
            sl = slice(gi * 4, gi * 4 + 4)
            ln2 = ln_chain(st_sx, st_sq, st_m, st_r, st_a, st_b, sl)
            vop(lambda e, sl=sl: e.tensor_tensor(out=st_mx[:, sl], in0=st_mx[:, sl],
                                          in1=st_m[:, sl], op=ALU.subtract), waits=[ln2])
            vop(lambda e, sl=sl: e.tensor_tensor(out=st_mn[:, sl], in0=st_m[:, sl],
                                          in1=st_mn[:, sl], op=ALU.subtract))
            vop(lambda e, sl=sl: e.tensor_tensor(out=st_mx[:, sl], in0=st_mx[:, sl],
                                          in1=st_mn[:, sl], op=ALU.max))
            vop(lambda e, sl=sl: e.tensor_tensor(out=st_mx[:, sl], in0=st_mx[:, sl],
                                          in1=st_r[:, sl], op=ALU.mult))
            mxc = vop(lambda e, sl=sl: e.tensor_single_scalar(out=st_mx[:, sl], in_=st_mx[:, sl],
                                                       scalar=1e-5, op=ALU.max))
            vop(lambda e, sl=sl: e.reciprocal(out=st_c1[:, sl], in_=st_mx[:, sl]), waits=[mxc])
            sxm = vop(lambda e, sl=sl: e.tensor_single_scalar(out=st_c1[:, sl], in_=st_c1[:, sl],
                                                       scalar=127.0, op=ALU.mult))
            vop(lambda e, sl=sl: e.tensor_tensor(out=st_a[:, sl], in0=st_r[:, sl],
                                          in1=st_c1[:, sl], op=ALU.mult), waits=[sxm])
            vop(lambda e, sl=sl: e.tensor_tensor(out=st_b[:, sl], in0=st_m[:, sl],
                                          in1=st_a[:, sl], op=ALU.mult))
            vop(lambda e, sl=sl: e.tensor_scalar(out=st_b[:, sl], in0=st_b[:, sl],
                                          scalar1=cst[:, 5:6], scalar2=cst[:, 0:1],
                                          op0=ALU.mult, op1=ALU.add), waits=[m_magic])
            vop(lambda e, sl=sl: e.tensor_single_scalar(out=st_mx[:, sl], in_=st_mx[:, sl],
                                                 scalar=1.0 / 127.0, op=ALU.mult))
            vop(lambda e, sl=sl: e.tensor_single_scalar(out=st_c1[:, sl], in_=st_mx[:, sl],
                                                 scalar=mcb[:, 0:1], op=ALU.mult))
            ln2q_marks[gi] = vop(lambda e, sl=sl: e.tensor_single_scalar(
                out=st_c2[:, sl], in_=st_mx[:, sl], scalar=mcb[:, 1:2], op=ALU.mult))

        hT_next = emit_hlnT(0)
        for w in range(NW):
            qs = w * 512
            hT_ready = hT_next

            q_marks = {}
            for f in range(16):
                buf = wsl[f % 4]
                src = wqkv_bf[:, f * 128:(f + 1) * 128].rearrange("(c p) j -> p c j", p=128)
                ldm = sch.dma("sync", buf[:].rearrange("p (c j) -> p c j", c=8), src,
                              waits=[slice_rd[f % 4]], stream=f"wsl{f % 4}")
                bank = ps[f % 2]
                mml = None
                for c in range(8):
                    mml = top(lambda e, bank=bank, buf=buf, c=c: e.matmul(
                        bank[:], buf[:, c * 128:(c + 1) * 128],
                        hlnT[:, c * 512:(c + 1) * 512],
                        start=(c == 0), stop=(c == 7)),
                        waits=[ldm, hT_ready, bank_rd[f % 2]] if c == 0 else [])
                slice_rd[f % 4] = mml
                if f < 8:
                    em = vop(lambda e, bank=bank, f=f: e.tensor_single_scalar(
                        out=QTw[:, f * 512:(f + 1) * 512], in_=bank[:],
                        scalar=bqkvT[:, f:f + 1], op=ALU.add), waits=[mml, bk_mark])
                    q_marks[f] = em
                else:
                    em = vop(lambda e, bank=bank, f=f, qs=qs: e.tensor_scalar(
                        out=KT[:, (f - 8) * S + qs:(f - 8) * S + qs + 512], in0=bank[:],
                        scalar1=cst[:, 1:2], scalar2=bqkvT[:, f:f + 1],
                        op0=ALU.mult, op1=ALU.add), waits=[mml, bk_mark])
                    kt_marks[(f - 8, w)] = em
                bank_rd[f % 2] = em

            for h in range(2):
                buf = wvs[h]
                src = wqkv_bf[:, 2048 + h * 512:2048 + (h + 1) * 512].rearrange(
                    "(c p) j -> p c j", p=128)
                ldm = sch.dma("sync", buf[:].rearrange("p (c j) -> p c j", c=8), src,
                              waits=[vslice_rd[0], vslice_rd[1]], stream="wvs")
                for j in range(4):
                    tt = w * 4 + j
                    bi = 2 + (j % 2)
                    bank = ps[bi]
                    mml = None
                    for c in range(8):
                        mml = top(lambda e, bank=bank, buf=buf, c=c, j=j: e.matmul(
                            bank[:], hlnT[:, c * 512 + j * 128:c * 512 + (j + 1) * 128],
                            buf[:, c * 512:(c + 1) * 512],
                            start=(c == 0), stop=(c == 7)),
                            waits=[ldm, hT_ready, bank_rd[bi]] if c == 0 else [])
                    em = aop(lambda e, bank=bank, tt=tt, h=h: e.activation(
                        out=VT[:, tt * 1024 + h * 512:tt * 1024 + (h + 1) * 512],
                        in_=bank[:], func=AF.Copy), waits=[mml])
                    vt_marks[(tt, h)] = em
                    bank_rd[bi] = em
                    vslice_rd[h] = mml

            if w + 1 < NW:
                hT_next = emit_hlnT(w + 1)
            if w > 0:
                emit_ln2_group(w - 1)

            if s_idx == 0 and w == 0:
                tap("d_QTw", QTw[:], waits=[q_marks[p_] for p_ in range(8)])
                tap("d_sq", st_sq[:], waits=[ln1_marks[NW - 1]])
                tap("d_sta", st_a[:], waits=[ln1_marks[NW - 1]])
            nkb = 4 * w + 4
            for p in range(8):
                exp_marks = {}
                pend = []
                vp = vps[p % 2]
                vbi = 4 + (p % 2)
                z1 = vop(lambda e, vp=vp: e.memset(vp[64:128, :], 0.0),
                         waits=[bank_rd[vbi]])
                z2 = vop(lambda e: e.memset(sg[32:33, :], 0.0),
                         waits=[bank_rd[6]])
                bank_rd[vbi] = z1
                bank_rd[6] = z2

                def emit_sv(kprev, offp):
                    pts = PT[kprev % 2]
                    st0 = (kprev == 0)
                    sp0_ = (kprev == nkb - 1)
                    top(lambda e, o=offp, b=pts, st=st0, sp=sp0_: e.matmul(
                        sg[0:1, o:512], ones_col[:], b[:, o:512],
                        start=st, stop=sp),
                        waits=[exp_marks[kprev]] +
                              ([bank_rd[6]] if kprev == 0 else []))
                    top(lambda e, o=offp, b=pts: e.matmul(
                        sg[32:33, o:512], ones_col[:], b[:, 512 + o:1024],
                        start=False, stop=False, skip_group_check=True), waits=[])
                    top(lambda e, o=offp, p=p, k=kprev, b=pts, st=st0, sp=sp0_, vp=vp: e.matmul(
                        vp[0:64, o:512],
                        VT[:, k * 1024 + 2 * p * 64:k * 1024 + (2 * p + 1) * 64],
                        b[:, o:512], start=st, stop=sp,
                        tile_position=(0, 0)),
                        waits=[vt_marks[(kprev, (2 * p * 64) // 512)],
                               vt_marks[(kprev, ((2 * p + 1) * 64) // 512)]] +
                              ([bank_rd[vbi]] if kprev == 0 else []))
                    vm_ = top(lambda e, o=offp, p=p, k=kprev, b=pts, vp=vp: e.matmul(
                        vp[64:128, o:512],
                        VT[:, k * 1024 + (2 * p + 1) * 64:k * 1024 + (2 * p + 2) * 64],
                        b[:, 512 + o:1024], start=False, stop=False,
                        tile_position=(0, 64), skip_group_check=True), waits=[])
                    pt_free[kprev % 2] = vm_
                    return vm_

                for kb in range(nkb):
                    off = max(0, kb * 128 - qs)
                    sb_t = sp[kb % 2]
                    straddle = kb * 128 >= qs
                    deps = [bank_rd[2 * (kb % 2)], bank_rd[2 * (kb % 2) + 1],
                            kt_marks[(p, kb // 4)], q_marks[p], hT_ready]
                    first = True
                    if straddle:
                        top(lambda e, b=sb_t, off=off: e.matmul(
                            b[:, off:off + 128], ident_bf[:], tri_bf[:],
                            start=True, stop=False), waits=deps)
                        top(lambda e, b=sb_t, off=off: e.matmul(
                            b[:, 512 + off:512 + off + 128], ident_bf[:], tri_bf[:],
                            start=True, stop=False), waits=[])
                        first = False
                        spans = [(off, off + 128, False)]
                        if off + 128 < 512:
                            spans.append((off + 128, 512, True))
                    else:
                        spans = [(off, 512, True)]
                    sc_m = None
                    for (o0, o1, st_flag) in spans:
                        top(lambda e, b=sb_t, o0=o0, o1=o1, p=p, kb=kb, st=st_flag: e.matmul(
                            b[:, o0:o1],
                            KT[0:64, p * S + kb * 128:p * S + (kb + 1) * 128],
                            QTw[0:64, p * 512 + o0:p * 512 + o1],
                            start=st, stop=True, tile_position=(0, 0)),
                            waits=deps if first else [])
                        first = False
                        sc_m = top(lambda e, b=sb_t, o0=o0, o1=o1, p=p, kb=kb, st=st_flag: e.matmul(
                            b[:, 512 + o0:512 + o1],
                            KT[64:128, p * S + kb * 128:p * S + (kb + 1) * 128],
                            QTw[64:128, p * 512 + o0:p * 512 + o1],
                            start=st, stop=True, tile_position=(64, 0)), waits=[])

                    if len(pend) == 2:
                        emit_sv(*pend.pop(0))
                    ew = [sc_m, pt_free[kb % 2]]
                    e1 = aop(lambda e, b=sb_t, off=off, o_=PT[kb % 2]: e.activation(
                        out=o_[:].rearrange("p (h q) -> p h q", h=2)[:, :, off:512],
                        in_=b[:].rearrange("p (h q) -> p h q", h=2)[:, :, off:512],
                        func=AF.Exp), waits=ew)
                    exp_marks[kb] = e1
                    bank_rd[2 * (kb % 2)] = e1
                    bank_rd[2 * (kb % 2) + 1] = e1
                    pend.append((kb, off))
                vals_mm = None
                while pend:
                    vals_mm = emit_sv(*pend.pop(0))

                l1 = aop(lambda e: e.activation(out=recf[0:1, :], in_=sg[0:1, :],
                                                func=AF.Ln), waits=[sch.snap("pe")])
                l2 = aop(lambda e: e.activation(out=recf[32:33, :], in_=sg[32:33, :],
                                                func=AF.Ln), waits=[])
                c1 = aop(lambda e: e.activation(out=recb[0:1, 0:512], in_=recf[0:1, :],
                                                func=AF.Exp, scale=-1.0),
                         waits=[recb_rd, l1])
                c2 = aop(lambda e: e.activation(out=recb[0:1, 512:1024], in_=recf[32:33, :],
                                                func=AF.Exp, scale=-1.0), waits=[l2])
                bm1 = top(lambda e: e.matmul(sg[0:64, :], onesr_bf[0:1, 0:64],
                                             recb[0:1, 0:512], start=True, stop=True,
                                             tile_position=(0, 0)), waits=[c1, c2])
                bm2 = top(lambda e: e.matmul(sg[64:128, :], onesr_bf[0:1, 0:64],
                                             recb[0:1, 512:1024], start=True, stop=True,
                                             tile_position=(0, 64)), waits=[])
                recb_rd = bm2
                be = aop(lambda e: e.activation(out=bcs[:], in_=sg[:], func=AF.Copy),
                         waits=[bm2, bcs_rd])
                bank_rd[6] = be
                vm = vop(lambda e, p=p, vp=vp: e.tensor_tensor(
                    out=valsT[:, p * 512:(p + 1) * 512], in0=vp[:], in1=bcs[:],
                    op=ALU.mult), waits=[vals_mm, be])
                bank_rd[vbi] = vm
                bcs_rd = vm
                if s_idx == 0 and w == 0 and p == 7:
                    tap("d_valsT", valsT[:], waits=[vm])
                    tap("d_recb", recb[:], waits=[vm])

            for jp in range(2):
                for jj in range(2):
                    j = jp * 2 + jj
                    tt = w * 4 + j
                    b0i = 0 + jj * 4
                    b1i = 1 + jj * 4
                    mml = None
                    for c in range(8):
                        mml = top(lambda e, c=c, j=j, b=ps[b0i]: e.matmul(
                            b[:], valsT[:, c * 512 + j * 128:c * 512 + (j + 1) * 128],
                            wo_sb[:, c * 1024:c * 1024 + 512],
                            start=(c == 0), stop=(c == 7)),
                            waits=[bank_rd[4], bank_rd[5], bank_rd[b0i], mk_wo] if c == 0 else [])
                    for c in range(8):
                        mml = top(lambda e, c=c, j=j, b=ps[b1i]: e.matmul(
                            b[:], valsT[:, c * 512 + j * 128:c * 512 + (j + 1) * 128],
                            wo_sb[:, c * 1024 + 512:(c + 1) * 1024],
                            start=(c == 0), stop=(c == 7)),
                            waits=[bank_rd[b1i]] if c == 0 else [])
                    ev0 = vop(lambda e, jp=jp, jj=jj, b=ps[b0i]: e.tensor_tensor(
                        out=attn[:, (jp * 2 + jj) * 1024:(jp * 2 + jj) * 1024 + 512], in0=b[:],
                        in1=beff[:, 0:512], op=ALU.add), waits=[mml])
                    ev1 = vop(lambda e, jp=jp, jj=jj, b=ps[b1i]: e.tensor_tensor(
                        out=attn[:, (jp * 2 + jj) * 1024 + 512:(jp * 2 + jj + 1) * 1024], in0=b[:],
                        in1=beff[:, 512:1024], op=ALU.add), waits=[])
                    bank_rd[b0i] = ev0
                    bank_rd[b1i] = ev1
                    vop(lambda e, jj=jj, jp=jp, w=w: e.tensor_reduce(
                        out=st_w[:, w * 4 + jp * 2 + jj:w * 4 + jp * 2 + jj + 1],
                        in_=attn[:, (jp * 2 + jj) * 1024:(jp * 2 + jj + 1) * 1024],
                        axis=AX.X, op=ALU.add), waits=[ev0, ev1])
                    aop(lambda e, jj=jj, jp=jp, w=w: e.activation(
                        out=sqj[:], in_=attn[:, (jp * 2 + jj) * 1024:(jp * 2 + jj + 1) * 1024],
                        func=AF.Square, accum_out=st_w2[:, w * 4 + jp * 2 + jj:w * 4 + jp * 2 + jj + 1]),
                        waits=[ev0, ev1])
            ln3 = ln_chain(st_w, st_w2, st_wm, st_wr, st_wa, st_wb,
                           slice(w * 4, w * 4 + 4))
            for j in range(4):
                tt = w * 4 + j
                t1 = vop(lambda e, j=j, w=w: e.tensor_scalar(
                    out=attn[:, j * 1024:(j + 1) * 1024],
                    in0=attn[:, j * 1024:(j + 1) * 1024],
                    scalar1=st_wa[:, w * 4 + j:w * 4 + j + 1],
                    scalar2=st_wb[:, w * 4 + j:w * 4 + j + 1],
                    op0=ALU.mult, op1=ALU.add), waits=[ln3])
                x1_marks[tt] = vop(lambda e, j=j, tt=tt: e.tensor_tensor(
                    out=x_sb[:, tt * 1024:(tt + 1) * 1024],
                    in0=attn[:, j * 1024:(j + 1) * 1024],
                    in1=x_sb[:, tt * 1024:(tt + 1) * 1024], op=ALU.add), waits=[t1])
            if w == NW - 1:
                emit_ln2_group(w)
            if s_idx == 0 and w == NW - 1:
                tap("d_KT", KT[:], waits=[kt_marks[(p_, NW - 1)] for p_ in range(8)])
                tap("d_VT", VT[:], waits=[vt_marks[(NT - 1, 0)], vt_marks[(NT - 1, 1)]])
                tap("d_x1", x_sb[:], waits=[x1_marks[tt_] for tt_ in range(NT)])
        sch.barrier()
        esA.close()

        esB = ExitStack()
        bb = lambda n, s_, d=F32: esB.enter_context(nc.sbuf_tensor(f"s{s_idx}b_{n}", s_, d))
        y_bf = bb("y_bf", [128, 4 * 4096], BF16)
        wbuf = bb("wbuf", [128, 16384], BF16)
        yqT = bb("yqT", [128, 16384], BF16)
        q1b = bb("q1b", [128, 1024], F32)
        q1c = bb("q1c", [128, 1024], F32)
        h2q = [bb(f"h2q{i}", [128, 1024], BF16) for i in range(2)]
        yq_t, yq_c = h2q[0], h2q[1]
        h2qT = bb("h2qT", [128, 8 * 512], BF16)
        silb0 = bb("silb0", [128, 512], F32)
        silb = [silb0, silb0]
        o_sb = bb("o_sb", [128, 4 * 1024], F32)
        sqj2 = sqj_g

        tr_free = None
        trB_free = None
        h2q_rd = [None, None]
        wbuf_rd = [None, None]
        bank_rd = [None] * 7
        sil_rd = [None]
        yq_rd = [None, None]
        ost_marks = {}
        trB = sp[0][:].bitcast(BF16)
        for g in range(4):
            a2d = {}
            trh = {}
            cph = {}

            def mk_a(j):
                tt = g * 4 + j
                hb = h2q[j % 2]
                a1 = vop(lambda e, tt=tt: e.tensor_scalar(
                    out=q1b[:], in0=x_sb[:, tt * 1024:(tt + 1) * 1024],
                    scalar1=st_a[:, tt:tt + 1], scalar2=st_b[:, tt:tt + 1],
                    op0=ALU.mult, op1=ALU.add), waits=[ln2q_marks[g]])
                a2d[j] = vop(lambda e, hb=hb: e.tensor_single_scalar(
                    out=hb[:], in_=q1b[:], scalar=cst[:, 0:1],
                    op=ALU.subtract), waits=[a1, h2q_rd[j % 2]])

            def mk_trh(j):
                hb = h2q[j % 2]
                tgt = tr_ps if j % 2 == 0 else trB
                free = tr_free if j % 2 == 0 else trB_free
                trs = None
                for c in range(8):
                    trs = top(lambda e, hb=hb, c=c, tgt=tgt: e.transpose(
                        tgt[:, c * 128:(c + 1) * 128], hb[:, c * 128:(c + 1) * 128],
                        ident_bf[:]),
                        waits=[a2d[j], free, bank_rd[0] if j % 2 else None]
                        if c == 0 else [])
                return trs

            def mk_cph(j):
                tgt = tr_ps if j % 2 == 0 else trB
                return vop(lambda e, j=j, tgt=tgt: e.tensor_copy(
                    out=h2qT[:].rearrange("p (c t) -> p c t", c=8)[:, :, j * 128:(j + 1) * 128],
                    in_=tgt[:, 0:1024].rearrange("p (c t) -> p c t", c=8)),
                    waits=[trh[j]])

            mk_a(0)
            mk_a(1)
            trh[0] = mk_trh(0)
            h2q_rd[0] = trh[0]
            mk_a(2)
            cph[0] = mk_cph(0)
            tr_free = cph[0]
            trh[1] = mk_trh(1)
            h2q_rd[1] = trh[1]
            mk_a(3)
            cph[1] = mk_cph(1)
            trB_free = cph[1]
            trh[2] = mk_trh(2)
            h2q_rd[0] = trh[2]
            cph[2] = mk_cph(2)
            tr_free = cph[2]
            trh[3] = mk_trh(3)
            h2q_rd[1] = trh[3]
            cph[3] = mk_cph(3)
            trB_free = cph[3]
            bank_rd[0] = cph[3]
            yq_rd[0] = trh[2]
            yq_rd[1] = trh[3]
            h2qT_ready = cph[3]

            y_marks = []
            red_marks = {}
            for iw in range(8):
                reg = iw % 2
                gsl = wbuf[:, reg * 8192:reg * 8192 + 4096]
                usl = wbuf[:, reg * 8192 + 4096:reg * 8192 + 8192]
                lg = sch.dma("sync", gsl.rearrange("p (c j) -> p c j", c=8),
                             wgq[:, iw * 512:(iw + 1) * 512].rearrange(
                                 "(c p) j -> p c j", p=128), waits=[wbuf_rd[reg]],
                             stream=f"gu{reg}")
                lu = sch.dma("sync", usl.rearrange("p (c j) -> p c j", c=8),
                             wuq[:, iw * 512:(iw + 1) * 512].rearrange(
                                 "(c p) j -> p c j", p=128), waits=[],
                             stream=f"gu{reg}")
                for j in range(4):
                    tt = g * 4 + j
                    par = j % 2
                    bg, bu = pb[par * 2], pb[par * 2 + 1]
                    sb_ = silb[par]
                    mml = None
                    for c in range(8):
                        mml = top(lambda e, bg=bg, gsl=gsl, c=c, j=j: e.matmul(
                            bg[:], h2qT[:, c * 512 + j * 128:c * 512 + (j + 1) * 128],
                            gsl[:, c * 512:(c + 1) * 512],
                            start=(c == 0), stop=(c == 7)),
                            waits=[lg, lu, h2qT_ready, bank_rd[par * 2]] if c == 0 else [])
                    for c in range(8):
                        mml = top(lambda e, bu=bu, usl=usl, c=c, j=j: e.matmul(
                            bu[:], h2qT[:, c * 512 + j * 128:c * 512 + (j + 1) * 128],
                            usl[:, c * 512:(c + 1) * 512],
                            start=(c == 0), stop=(c == 7)),
                            waits=[bank_rd[par * 2 + 1]] if c == 0 else [])
                    wbuf_rd[reg] = mml
                    sm = aop(lambda e, bg=bg, tt=tt, sb_=sb_: e.activation(
                        out=sb_[:], in_=bg[:], func=AF.Silu,
                        scale=st_c1[:, tt:tt + 1]), waits=[mml, sil_rd[0], ln2q_marks[g]])
                    ym = vop(lambda e, bu=bu, tt=tt, j=j, iw=iw, sb_=sb_: e.scalar_tensor_tensor(
                        out=y_bf[:, j * 4096 + iw * 512:j * 4096 + (iw + 1) * 512],
                        in0=bu[:], scalar=st_c2[:, tt:tt + 1], in1=sb_[:],
                        op0=ALU.mult, op1=ALU.mult), waits=[sm])
                    rm = vop(lambda e, j=j, iw=iw: e.tensor_reduce(
                        out=st_p[:, j * 8 + iw:j * 8 + iw + 1],
                        in_=y_bf[:, j * 4096 + iw * 512:j * 4096 + (iw + 1) * 512],
                        axis=AX.X, op=ALU.max, apply_absolute_value=True), waits=[ym])
                    red_marks[(j, iw)] = rm
                    sil_rd[0] = ym
                    bank_rd[par * 2] = sm
                    bank_rd[par * 2 + 1] = ym
                    y_marks.append(ym)
            y_done = y_marks[-1]
            if s_idx == 0 and g == 0:
                tap("d_h2qT", h2qT[:], waits=[h2qT_ready])
                tap("d_y", y_bf[:], waits=[y_done])

            cd_marks = {}
            sy_marks = {}
            for j in range(4):
                tt = g * 4 + j
                c0m = vop(lambda e, j=j, tt=tt: e.tensor_reduce(
                    out=st_my[:, tt:tt + 1], in_=st_p[:, j * 8:(j + 1) * 8],
                    axis=AX.X, op=ALU.max), waits=[red_marks[(j, 7)]])
                c1m = vop(lambda e, tt=tt: e.tensor_single_scalar(
                    out=st_my[:, tt:tt + 1], in_=st_my[:, tt:tt + 1],
                    scalar=1e-5, op=ALU.max))
                c2m = vop(lambda e, tt=tt: e.reciprocal(
                    out=st_cd[:, tt:tt + 1], in_=st_my[:, tt:tt + 1]), waits=[c1m])
                sy_marks[j] = vop(lambda e, tt=tt: e.tensor_single_scalar(
                    out=st_cd[:, tt:tt + 1], in_=st_cd[:, tt:tt + 1],
                    scalar=127.0, op=ALU.mult), waits=[c2m])
                vop(lambda e, tt=tt: e.tensor_single_scalar(
                    out=st_my[:, tt:tt + 1], in_=st_my[:, tt:tt + 1],
                    scalar=1.0 / 127.0, op=ALU.mult), waits=[sy_marks[j]])
                cd_marks[j] = vop(lambda e, tt=tt: e.tensor_single_scalar(
                    out=st_my[:, tt:tt + 1], in_=st_my[:, tt:tt + 1],
                    scalar=mcb[:, 2:3], op=ALU.mult))
            cdm = cd_marks[3]

            def gop(fn, waits=(), inc="gp"):
                return sch.op("gpsimd", fn, waits=waits, inc=inc)

            tq_marks = {}
            dn_ev = {}
            wd_rd = None
            lw0 = sch.dma("sync", wbuf[:].rearrange("p (c j) -> p c j", c=32),
                          wdq[:, 0:512].rearrange("(c p) j -> p c j", p=128),
                          waits=[wbuf_rd[0], wbuf_rd[1]], stream="wd")

            def emit_rt(qt):
                nonlocal yq_rd, tr_free, trB_free
                r1 = {}
                rl = {}
                tr = {}
                cp = {}

                def mk_r1(j, extra):
                    tt = g * 4 + j
                    qb = q1b if (qt + j) % 2 == 0 else q1c
                    return aop(lambda e, j=j, qt=qt, tt=tt, qb=qb: e.activation(
                        out=qb[:],
                        in_=y_bf[:, j * 4096 + qt * 1024:j * 4096 + (qt + 1) * 1024],
                        func=AF.Copy, scale=st_cd[:, tt:tt + 1]),
                        waits=[sy_marks[j], extra])

                def mk_rl(j, extra):
                    qb = q1b if (qt + j) % 2 == 0 else q1c
                    yqt = yq_t if (qt + j) % 2 == 0 else yq_c
                    return vop(lambda e, qb=qb, yqt=yqt: e.tensor_scalar(
                        out=yqt[:], in0=qb[:],
                        scalar1=cst[:, 0:1], scalar2=cst[:, 0:1],
                        op0=ALU.add, op1=ALU.subtract),
                        waits=[r1[j], extra])

                def mk_tr(j):
                    yqt = yq_t if (qt + j) % 2 == 0 else yq_c
                    tgt = tr_ps if j % 2 == 0 else trB
                    free = tr_free if j % 2 == 0 else trB_free
                    trs = None
                    for c in range(8):
                        trs = top(lambda e, c=c, yqt=yqt, tgt=tgt: e.transpose(
                            tgt[:, c * 128:(c + 1) * 128],
                            yqt[:, c * 128:(c + 1) * 128], ident_bf[:]),
                            waits=[rl[j], free, bank_rd[0] if j % 2 else None]
                            if c == 0 else [])
                    return trs

                def mk_cp(j):
                    tgt = tr_ps if j % 2 == 0 else trB
                    return vop(lambda e, j=j, qt=qt, tgt=tgt: e.tensor_copy(
                        out=yqT[:].rearrange("p (c t) -> p c t", c=32)
                            [:, qt * 8:(qt + 1) * 8, j * 128:(j + 1) * 128],
                        in_=tgt[:, 0:1024].rearrange("p (c t) -> p c t", c=8)),
                        waits=[tr[j]])

                r1[0] = mk_r1(0, yq_rd[qt % 2])
                r1[1] = mk_r1(1, yq_rd[(qt + 1) % 2])
                rl[0] = mk_rl(0, yq_rd[qt % 2])
                rl[1] = mk_rl(1, yq_rd[(qt + 1) % 2])
                tr[0] = mk_tr(0)
                yq_rd[qt % 2] = tr[0]
                r1[2] = mk_r1(2, rl[0])
                rl[2] = mk_rl(2, tr[0])
                cp[0] = mk_cp(0)
                tr_free = cp[0]
                tr[1] = mk_tr(1)
                yq_rd[(qt + 1) % 2] = tr[1]
                r1[3] = mk_r1(3, rl[1])
                rl[3] = mk_rl(3, tr[1])
                cp[1] = mk_cp(1)
                trB_free = cp[1]
                tr[2] = mk_tr(2)
                yq_rd[qt % 2] = tr[2]
                cp[2] = mk_cp(2)
                tr_free = cp[2]
                tr[3] = mk_tr(3)
                yq_rd[(qt + 1) % 2] = tr[3]
                cp[3] = mk_cp(3)
                trB_free = cp[3]
                bank_rd[0] = cp[3]
                return cp[3]

            def emit_down_chunk(hw, tp, ic0, ic1, lw):
                mml = None
                for ic in range(ic0, ic1):
                    for t2 in range(2):
                        j = tp * 2 + t2
                        bank = pb[2 + t2]
                        mmw = []
                        if ic == 0:
                            mmw = [lw, tq_marks[0], bank_rd[2 + t2],
                                   dn_ev.get((hw - 1, j)), dn_ev.get((hw, j - 2))]
                        elif ic % 8 == 0:
                            mmw = [tq_marks[ic // 8]]
                        mml = top(lambda e, ic=ic, j=j, bank=bank: e.matmul(
                            bank[:], yqT[:, ic * 512 + j * 128:ic * 512 + (j + 1) * 128],
                            wbuf[:, ic * 512:(ic + 1) * 512],
                            start=(ic == 0), stop=(ic == 31)), waits=mmw)
                return mml

            def emit_down_evac(hw, tp):
                pe_done = sch.snap("pe")
                for t2 in range(2):
                    j = tp * 2 + t2
                    tt = g * 4 + j
                    dn_ev[(hw, j)] = vop(lambda e, j=j, hw=hw, tt=tt, t2=t2: e.tensor_single_scalar(
                        out=o_sb[:, j * 1024 + hw * 512:j * 1024 + (hw + 1) * 512],
                        in_=pb[2 + t2][:], scalar=st_my[:, tt:tt + 1], op=ALU.mult),
                        waits=[pe_done, cd_marks[j], ost_marks.get(j)])
                    bank_rd[2 + t2] = dn_ev[(hw, j)]

            for qt in range(4):
                tq_marks[qt] = emit_rt(qt)
                wd_rd = emit_down_chunk(0, 0, qt * 8, (qt + 1) * 8, lw0)
            h2q_rd[0] = yq_rd[0]
            h2q_rd[1] = yq_rd[1]
            emit_down_evac(0, 0)
            wd_rd = emit_down_chunk(0, 1, 0, 32, lw0)
            emit_down_evac(0, 1)
            lw1 = sch.dma("sync", wbuf[:].rearrange("p (c j) -> p c j", c=32),
                          wdq[:, 512:1024].rearrange("(c p) j -> p c j", p=128),
                          waits=[wd_rd], stream="wd")
            wd_rd = emit_down_chunk(1, 0, 0, 32, lw1)
            emit_down_evac(1, 0)
            wd_rd = emit_down_chunk(1, 1, 0, 32, lw1)
            emit_down_evac(1, 1)
            wbuf_rd = [wd_rd, wd_rd]

            for j in range(4):
                tt = g * 4 + j
                w0 = [dn_ev[(0, j)], dn_ev[(1, j)]]
                vop(lambda e, tt=tt, j=j: e.tensor_reduce(
                    out=st_w[:, tt:tt + 1], in_=o_sb[:, j * 1024:(j + 1) * 1024],
                    axis=AX.X, op=ALU.add), waits=w0)
                aop(lambda e, tt=tt, j=j: e.activation(
                    out=sqj2[:], in_=o_sb[:, j * 1024:(j + 1) * 1024], func=AF.Square,
                    accum_out=st_w2[:, tt:tt + 1]), waits=w0)
            ln4 = ln_chain(st_w, st_w2, st_wm, st_wr, st_wa, st_wb,
                           slice(g * 4, g * 4 + 4))
            for j in range(4):
                tt = g * 4 + j
                t1 = vop(lambda e, tt=tt, j=j: e.tensor_scalar(
                    out=o_sb[:, j * 1024:(j + 1) * 1024],
                    in0=o_sb[:, j * 1024:(j + 1) * 1024],
                    scalar1=st_wa[:, tt:tt + 1], scalar2=st_wb[:, tt:tt + 1],
                    op0=ALU.mult, op1=ALU.add), waits=[ln4])
                om = vop(lambda e, j=j, tt=tt: e.tensor_tensor(
                    out=o_sb[:, j * 1024:(j + 1) * 1024],
                    in0=o_sb[:, j * 1024:(j + 1) * 1024],
                    in1=x_sb[:, tt * 1024:(tt + 1) * 1024], op=ALU.add), waits=[t1])
                ost_marks[j] = sch.dma("gpsimd", out_hbm[s_idx, tt * 128:(tt + 1) * 128, :],
                        o_sb[:, j * 1024:(j + 1) * 1024], waits=[om], stream=f"out{j}")
                if s_idx + 1 < NSEQ:
                    next_xmarks[tt] = sch.dma(
                        "gpsimd", x_sb[:, tt * 1024:(tt + 1) * 1024],
                        x_in[s_idx + 1, tt * 128:(tt + 1) * 128, :],
                        waits=[om], stream=f"xn{tt}")
                if s_idx == 0 and g == 0 and j == 3:
                    tap("d_osb", o_sb[:], waits=[om])
        sch.barrier()
        esB.close()
        return next_xmarks

    nxt = None
    for s_idx in range(NSEQ):
        nxt = emit_seq(s_idx, nxt)
    sch.barrier()

    with nc.Block() as block:
        sch.build(block)
    return nc


def kernel(L0_attn_x, L0_attn_W_qkv, L0_attn_b_qkv, L0_attn_W_out, L0_attn_b_out,
           L0_ffn_w_gate, L0_ffn_w_up, L0_ffn_w_down):
    x = np.ascontiguousarray(np.asarray(L0_attn_x, dtype=np.float32))
    in_common = {
        "wqkv": np.ascontiguousarray(np.asarray(L0_attn_W_qkv, np.float32)),
        "bqkv": np.ascontiguousarray(np.asarray(L0_attn_b_qkv, np.float32)),
        "wout": np.ascontiguousarray(np.asarray(L0_attn_W_out, np.float32)),
        "bout": np.ascontiguousarray(np.asarray(L0_attn_b_out, np.float32)),
        "wgT": np.ascontiguousarray(np.asarray(L0_ffn_w_gate, np.float32).T),
        "wuT": np.ascontiguousarray(np.asarray(L0_ffn_w_up, np.float32).T),
        "wdT": np.ascontiguousarray(np.asarray(L0_ffn_w_down, np.float32).T),
    }
    nc = build_program()
    in_maps = []
    for c in range(NCORES):
        m = dict(in_common)
        m["x"] = np.ascontiguousarray(x[c * NSEQ:(c + 1) * NSEQ])
        in_maps.append(m)
    res = run_bass_kernel_spmd(nc, in_maps, core_ids=list(range(NCORES)))
    return np.concatenate([r["out"] for r in res.results], axis=0)


# version 30
# speedup vs baseline: 1.0049x; 1.0049x over previous
import numpy as np
import ml_dtypes
from contextlib import ExitStack

import concourse.bass as bass
import concourse.mybir as mybir
from concourse import dve_ops as _dve_ops
from concourse.bass_utils import run_bass_kernel_spmd

BF = ml_dtypes.bfloat16
F32 = mybir.dt.float32
BF16 = mybir.dt.bfloat16
AF = mybir.ActivationFunctionType
ALU = mybir.AluOpType
AX = mybir.AxisListType

NCORES = 8
NSEQ = 2
S = 2048
D = 1024
INNER = 4096
NT = S // 128
NW = S // 512
EPS = 1e-5

ENGS = ("sync", "tensor", "scalar", "vector", "gpsimd")
_DEBUG = False


class Sched:
    def __init__(self, nc):
        self.nc = nc
        self.sems = {}
        self.cnt = {}
        self.items = {e: [] for e in ENGS}
        self.hwm = {e: {} for e in ENGS}
        self.dma_streams = {}

    def sem(self, name):
        if name not in self.sems:
            self.sems[name] = self.nc.alloc_semaphore(name)
            self.cnt[name] = 0
        return self.sems[name]

    def op(self, eng, fn, waits=(), inc=None, dk=1):
        waits = list(waits)
        if inc in ("dve", "act", "gp"):
            self.sem(inc)
            waits.append((inc, self.cnt[inc]))
        merged = {}
        for wv in waits:
            if wv is None:
                continue
            s, v = wv
            if v is None or v <= 0:
                continue
            if v > merged.get(s, 0):
                merged[s] = v
        ws = []
        for s, v in merged.items():
            self.sem(s)
            if v > self.hwm[eng].get(s, 0):
                ws.append((s, v))
                self.hwm[eng][s] = v
        mark = None
        if inc is not None:
            self.sem(inc)
            self.cnt[inc] += dk
            mark = (inc, self.cnt[inc])
        self.items[eng].append((ws, fn, inc, dk))
        return mark

    def dma(self, eng, out_ap, in_ap, waits=(), stream="misc"):
        sname = "D_" + stream
        self.dma_streams.setdefault(eng, set()).add(sname)
        return self.op(eng, lambda e, o=out_ap, i=in_ap: e.dma_start(out=o, in_=i),
                       waits=waits, inc=sname, dk=16)

    def snap(self, name):
        self.sem(name)
        return (name, self.cnt[name])

    def barrier(self):
        for e in ENGS:
            w = [(s, self.cnt[s]) for s in self.dma_streams.get(e, ())]
            self.op(e, lambda eng: eng.nop(), waits=w, inc="bar")
        tgt = self.cnt["bar"]
        for e in ENGS:
            self.op(e, None, waits=[("bar", tgt)])
        return ("bar", tgt)

    def build(self, block):
        for ename in ENGS:
            items = self.items[ename]

            def body(eng, items=items):
                for ws, fn, inc, dk in items:
                    for s, v in ws:
                        eng.wait_ge(self.sems[s], v)
                    if fn is None:
                        continue
                    inst = fn(eng)
                    if inc is not None and inst is not None:
                        inst.then_inc(self.sems[inc], dk)

            getattr(block, ename)(body)


def build_program():
    nc = bass.Bass("TRN2", target_bir_lowering=False)

    x_in = nc.dram_tensor("x", [NSEQ, S, D], F32, kind="ExternalInput").ap()
    wqkv = nc.dram_tensor("wqkv", [D, 3 * D], F32, kind="ExternalInput").ap()
    bqkv = nc.dram_tensor("bqkv", [3 * D], F32, kind="ExternalInput").ap()
    wout = nc.dram_tensor("wout", [D, D], F32, kind="ExternalInput").ap()
    bout = nc.dram_tensor("bout", [D], F32, kind="ExternalInput").ap()
    wgT = nc.dram_tensor("wgT", [D, INNER], F32, kind="ExternalInput").ap()
    wuT = nc.dram_tensor("wuT", [D, INNER], F32, kind="ExternalInput").ap()
    wdT = nc.dram_tensor("wdT", [INNER, D], F32, kind="ExternalInput").ap()
    out_hbm = nc.dram_tensor("out", [NSEQ, S, D], F32, kind="ExternalOutput").ap()
    dbg = {}
    if _DEBUG:
        for nm, shp, dt in [("d_KT", [128, 8 * S], "bf"), ("d_VT", [128, NT * 1024], "bf"),
                            ("d_QTw", [128, 8 * 512], "bf"), ("d_valsT", [128, 8 * 512], "bf"),
                            ("d_x1", [128, NT * 1024], "f"), ("d_h2qT", [128, 8 * 512], "bf"),
                            ("d_y", [128, 4 * 4096], "bf"), ("d_osb", [128, 4 * 1024], "f"),
                            ("d_wq", [128, 4096], "bf"), ("d_sq", [128, NT], "f"),
                            ("d_ffa", [128, NT], "f"), ("d_ffb", [128, NT], "f"),
                            ("d_ffc1", [128, NT], "f"), ("d_ffmx", [128, NT], "f"),
                            ("d_ffr", [128, NT], "f"), ("d_ffm", [128, NT], "f"),
                            ("d_sta", [128, NT], "f"), ("d_recb", [1, 1024], "bf")]:
            dbg[nm] = nc.dram_tensor(nm, shp, BF16 if dt == "bf" else F32,
                                     kind="ExternalOutput").ap()
    wqkv_bf = nc.dram_tensor("wqkv_bf", [D, 3 * D], BF16, kind="Internal").ap()
    wgq = nc.dram_tensor("wgq", [D, INNER], BF16, kind="Internal").ap()
    wuq = nc.dram_tensor("wuq", [D, INNER], BF16, kind="Internal").ap()
    wdq = nc.dram_tensor("wdq", [INNER, D], BF16, kind="Internal").ap()

    ident_bf_h = nc.inline_tensor(np.eye(128, dtype=BF), name="c_ident_bf")
    ident_f_h = nc.inline_tensor(np.eye(128, dtype=np.float32), name="c_ident_f")
    tri_np = np.where(np.arange(128)[:, None] > np.arange(128)[None, :],
                      np.float32(-30000.0), np.float32(0.0)).astype(BF)
    tri_h = nc.inline_tensor(tri_np, name="c_tri")

    sch = Sched(nc)
    es_root = ExitStack()
    sb = lambda n, s_, d=F32: es_root.enter_context(nc.sbuf_tensor(n, s_, d))

    ident_bf = sb("ident_bf", [128, 128], BF16)
    tri_bf = sb("tri_bf", [128, 128], BF16)
    ones_col = sb("ones_col", [128, 1], BF16)
    onesr_bf = sb("onesr_bf", [1, 128], BF16)
    onesr_f = sb("onesr_f", [1, 128], F32)
    cst = sb("cst", [128, 8], F32)
    bqkvT = sb("bqkvT", [128, 24], F32)
    beff = sb("beff", [128, 1024], BF16)
    sqj_g = sb("sqj_g", [128, 1024], BF16)
    mcb = sb("mcb", [128, 4], F32)
    thrb = sb("thrb", [128, 4], F32)
    wsums = sb("wsums", [128, 48], F32)
    wtot = sb("wtot", [1, 8], F32)
    st_sx = sb("st_sx", [128, NT], F32)
    st_sq = sb("st_sq", [128, NT], F32)
    st_mx = sb("st_mx", [128, NT], F32)
    st_mn = sb("st_mn", [128, NT], F32)
    st_m = sb("st_m", [128, NT], F32)
    st_r = sb("st_r", [128, NT], F32)
    st_a = sb("st_a", [128, NT], F32)
    st_b = sb("st_b", [128, NT], F32)
    st_c1 = sb("st_c1", [128, NT], F32)
    st_c2 = sb("st_c2", [128, NT], F32)
    st_my = sb("st_my", [128, NT], F32)
    st_p = sb("st_p", [128, 32], F32)
    st_cd = sb("st_cd", [128, NT], F32)
    st_w = sb("st_w", [128, 16], F32)
    st_w2 = sb("st_w2", [128, 16], F32)
    st_wa = sb("st_wa", [128, 16], F32)
    st_wb = sb("st_wb", [128, 16], F32)
    st_wm = sb("st_wm", [128, 16], F32)
    st_wr = sb("st_wr", [128, 16], F32)
    x_sb = sb("x_sb", [128, NT * 1024], F32)

    def tap(name, sbuf_ap, waits=()):
        if _DEBUG and name in dbg:
            return sch.dma("sync", dbg[name], sbuf_ap, waits=waits, stream="tap")
        return None

    def vop(fn, waits=(), inc="dve"):
        return sch.op("vector", fn, waits=waits, inc=inc)

    def aop(fn, waits=(), inc="act"):
        return sch.op("scalar", fn, waits=waits, inc=inc)

    def top(fn, waits=(), inc="pe"):
        return sch.op("tensor", fn, waits=waits, inc=inc)

    def ln_chain(sx, sq, m_t, r_t, a_t, b_t, s_, denom=1024.0):
        act_now = sch.snap("act")
        m1 = vop(lambda e: e.tensor_single_scalar(
            out=m_t[:, s_], in_=sx[:, s_], scalar=1.0 / denom, op=ALU.mult),
            waits=[act_now])
        vop(lambda e: e.tensor_tensor(
            out=b_t[:, s_], in0=m_t[:, s_], in1=m_t[:, s_], op=ALU.mult))
        m3a = vop(lambda e: e.scalar_tensor_tensor(
            out=r_t[:, s_], in0=sq[:, s_], scalar=1.0 / denom, in1=b_t[:, s_],
            op0=ALU.mult, op1=ALU.subtract))
        m3 = vop(lambda e: e.tensor_single_scalar(
            out=r_t[:, s_], in_=r_t[:, s_], scalar=0.0, op=ALU.max))
        a1 = aop(lambda e: e.activation(
            out=r_t[:, s_], in_=r_t[:, s_], func=AF.Ln, bias=cst[:, 2:3], scale=1.0), waits=[m3])
        a2 = aop(lambda e: e.activation(
            out=r_t[:, s_], in_=r_t[:, s_], func=AF.Exp, bias=0.0, scale=-0.5), waits=[a1])
        m4 = vop(lambda e: e.tensor_copy(out=a_t[:, s_], in_=r_t[:, s_]), waits=[a2])
        m5 = vop(lambda e: e.scalar_tensor_tensor(
            out=b_t[:, s_], in0=m_t[:, s_], scalar=-1.0, in1=a_t[:, s_],
            op0=ALU.mult, op1=ALU.mult), waits=[m4])
        return m5

    es_prep_sb = ExitStack()
    ident_f = es_prep_sb.enter_context(nc.sbuf_tensor("ident_f", [128, 128], F32))
    b24 = es_prep_sb.enter_context(nc.sbuf_tensor("b24", [24, 128], F32))
    sch.dma("sync", ident_bf[:], ident_bf_h.ap(), stream="cst")
    sch.dma("sync", ident_f[:], ident_f_h.ap(), stream="cst")
    sch.dma("sync", tri_bf[:], tri_h.ap(), stream="cst")
    sch.dma("sync", b24[:], bqkv.rearrange("(c p) -> c p", p=128), stream="cst")
    ld0 = sch.snap("D_cst")

    vop(lambda e: e.memset(ones_col[:], 1.0))
    vop(lambda e: e.memset(onesr_bf[:], 1.0))
    vop(lambda e: e.memset(onesr_f[:], 1.0))
    vop(lambda e: e.memset(cst[:, 1:2], 0.125))
    vop(lambda e: e.memset(cst[:, 2:3], EPS))
    vop(lambda e: e.memset(cst[:, 3:4], 1.0))
    vop(lambda e: e.memset(cst[:, 5:6], -1.0))
    vop(lambda e: e.memset(cst[:, 4:5], 1.0))
    vop(lambda e: e.tensor_scalar(out=cst[:, 4:5], in0=cst[:, 4:5], scalar1=2.0 ** -30,
                                  scalar2=1.0, op0=ALU.add, op1=ALU.subtract))
    m_magic = vop(lambda e: e.tensor_scalar(
        out=cst[:, 0:1], in0=cst[:, 4:5],
        scalar1=2.0 ** 30 * (1.5 * 2.0 ** 52 - 1.5 * 2.0 ** 23), scalar2=1.5 * 2.0 ** 23,
        op0=ALU.mult, op1=ALU.add))

    es_prep_ps = ExitStack()
    pbias = es_prep_ps.enter_context(nc.psum_tensor("pbias", [128, 24], F32))
    trm = top(lambda e: e.transpose(pbias[:], b24[:], ident_f[0:24, 0:24]), waits=[ld0])
    evm = vop(lambda e: e.tensor_copy(out=bqkvT[:], in_=pbias[:]), waits=[trm])
    bk_mark = vop(lambda e: e.tensor_single_scalar(
        out=bqkvT[:, 8:16], in_=bqkvT[:, 8:16], scalar=0.125, op=ALU.mult), waits=[evm])

    sch.barrier()

    prep = ExitStack()
    psb = lambda n, s_, d=F32: prep.enter_context(nc.sbuf_tensor(n, s_, d))
    wf = [psb(f"wf{i}", [128, 4096], F32) for i in range(3)]
    wjunk = psb("wjunk", [128, 4096], BF16)
    wabs = [psb(f"wabs{i}", [128, 4096], F32) for i in range(2)]
    wsgn = [psb(f"wsgn{i}", [128, 4096], BF16) for i in range(2)]
    wqb = [psb(f"wqb{i}", [128, 4096], BF16) for i in range(2)]
    onescol_f = psb("onescol_f", [128, 1], F32)
    vop(lambda e: e.memset(onescol_f[:], 1.0))

    for cb in range(6):
        sch.dma("gpsimd", wqkv_bf[:, cb * 512:(cb + 1) * 512],
                wqkv[:, cb * 512:(cb + 1) * 512], stream=f"wqc{cb}")

    mats = [(wgT, wgq, 8, 4096), (wuT, wuq, 8, 4096), (wdT, wdq, 32, 1024)]
    col = 0
    tmarks = []
    for mi, (wsrc, wdst, ntile, wid) in enumerate(mats):
        for t in range(ntile):
            buf = wf[col % 3]
            w = [tmarks[-3]] if len(tmarks) >= 3 else []
            ldm = sch.dma("sync", buf[:, 0:wid], wsrc[t * 128:(t + 1) * 128, :],
                          waits=w, stream=f"wf{col % 3}")
            am = aop(lambda e, b=buf, wd=wid, c=col: e.activation(
                out=wjunk[:, 0:wd], in_=b[:, 0:wd], func=AF.Abs,
                accum_out=wsums[:, c:c + 1]), waits=[ldm])
            tmarks.append(am)
            col += 1
    ncols = col

    ptot = es_prep_ps.enter_context(nc.psum_tensor("ptot", [1, 64], F32))
    pbc = es_prep_ps.enter_context(nc.psum_tensor("pbc", [128, 8], F32))
    tm = top(lambda e: e.matmul(ptot[0:1, 0:ncols], onescol_f[:], wsums[:, 0:ncols],
                                start=True, stop=True),
             waits=[tmarks[-1], sch.snap("dve")])
    c0 = 0
    redm = []
    for mi, (wsrc, wdst, ntile, wid) in enumerate(mats):
        rm = vop(lambda e, a=c0, b=c0 + ntile, i=mi: e.tensor_reduce(
            out=wtot[0:1, i:i + 1], in_=ptot[0:1, a:b], axis=AX.X, op=ALU.add), waits=[tm])
        redm.append(rm)
        c0 += ntile
    scm = []
    for mi, (wsrc, wdst, ntile, wid) in enumerate(mats):
        n_el = float(ntile * 128 * wid)
        scm.append(vop(lambda e, i=mi, ne=n_el: e.tensor_scalar(
            out=wtot[0:1, i:i + 1], in0=wtot[0:1, i:i + 1],
            scalar1=1.0 / ne, scalar2=1e-5, op0=ALU.mult, op1=ALU.max), waits=[redm[mi]]))
    thrm = vop(lambda e: e.tensor_single_scalar(
        out=wtot[0:1, 4:7], in_=wtot[0:1, 0:3], scalar=0.5, op=ALU.mult), waits=scm)
    bc1 = top(lambda e: e.matmul(pbc[:, 0:3], onesr_f[:], wtot[0:1, 0:3],
                                 start=True, stop=True), waits=[thrm])
    bc2 = top(lambda e: e.matmul(pbc[:, 4:7], onesr_f[:], wtot[0:1, 4:7],
                                 start=True, stop=True), waits=[thrm])
    vop(lambda e: e.tensor_copy(out=mcb[:, 0:3], in_=pbc[:, 0:3]), waits=[bc1, bc2])
    vop(lambda e: e.tensor_copy(out=thrb[:, 0:3], in_=pbc[:, 4:7]), waits=[bc1, bc2])
    sch.barrier()

    prevq = []
    last_qm = {}
    st_marks = []
    col = 0
    for mi, (wsrc, wdst, ntile, wid) in enumerate(mats):
        for t in range(ntile):
            buf = wf[col % 3]
            par = col % 2
            w = [prevq[-3]] if len(prevq) >= 3 else []
            ldm = sch.dma("sync", buf[:, 0:wid], wsrc[t * 128:(t + 1) * 128, :],
                          waits=w, stream=f"wf{col % 3}")
            a1 = aop(lambda e, b=buf, wd=wid, par=par: e.activation(
                out=wabs[par][:, 0:wd], in_=b[:, 0:wd], func=AF.Abs),
                waits=[ldm, last_qm.get(par)])
            a2 = aop(lambda e, b=buf, wd=wid, par=par: e.activation(
                out=wsgn[par][:, 0:wd], in_=b[:, 0:wd], func=AF.Sign), waits=[])
            qm = vop(lambda e, wd=wid, i=mi, par=par: e.scalar_tensor_tensor(
                out=wqb[par][:, 0:wd], in0=wabs[par][:, 0:wd], scalar=thrb[:, i:i + 1],
                in1=wsgn[par][:, 0:wd], op0=ALU.is_gt, op1=ALU.mult),
                waits=[a1, a2, st_marks[-2] if len(st_marks) >= 2 else None])
            last_qm[par] = qm
            stm = sch.dma("gpsimd", wdst[t * 128:(t + 1) * 128, :], wqb[par][:, 0:wid],
                          waits=[qm], stream=f"wst{par}")
            st_marks.append(stm)
            prevq.append(qm)
            if mi == 0 and t == 0:
                tpm = tap("d_wq", wqb[par][:], waits=[qm])
                if tpm:
                    st_marks.append(tpm)
            col += 1
    prevq = st_marks
    sch.barrier()
    prep.close()
    es_prep_sb.close()
    es_prep_ps.close()

    sp = [es_root.enter_context(nc.psum_tensor(f"sp{i}", [128, 1024], F32)) for i in range(2)]
    vps = [es_root.enter_context(nc.psum_tensor(f"vps{i}", [128, 512], F32)) for i in range(2)]
    sg = es_root.enter_context(nc.psum_tensor("sg", [128, 512], F32))
    tr_ps = es_root.enter_context(nc.psum_tensor("tr_ps", [128, 1024], BF16))
    pb = [sp[0][:, 0:512], sp[0][:, 512:1024], sp[1][:, 0:512], sp[1][:, 512:1024]]
    ps = pb + [vps[0][:], vps[1][:], sg[:]]

    def emit_seq(s_idx, xmarks=None):
        esA = ExitStack()
        ab = lambda n, s_, d=F32: esA.enter_context(nc.sbuf_tensor(f"s{s_idx}_{n}", s_, d))
        KT = ab("KT", [128, 8 * S], BF16)
        VT = ab("VT", [128, NT * 1024], BF16)
        QTw = ab("QTw", [128, 8 * 512], BF16)
        hln0 = ab("hln0", [128, 1024], BF16)
        hln = [hln0, hln0]
        hlnT = ab("hlnT", [128, 8 * 512], BF16)
        wsl = [ab(f"wsl{i}", [128, 8 * 128], BF16) for i in range(4)]
        wvs0 = ab("wvs0", [128, 8 * 512], BF16)
        wvs = [wvs0, wvs0]
        wo_sb = ab("wo_sb", [128, 8 * 1024], BF16)
        PT = [ab(f"PT{i}", [128, 1024], BF16) for i in range(2)]
        recf = ab("recf", [128, 512], F32)
        recb = ab("recb", [1, 1024], BF16)
        bcs = ab("bcs", [128, 512], F32)
        valsT = ab("valsT", [128, 8 * 512], BF16)
        attn = QTw
        sqj = sqj_g

        next_xmarks = {}
        if xmarks is None:
            mk_xt = {}
            for tt in range(NT):
                mk_xt[tt] = sch.dma("sync", x_sb[:, tt * 1024:(tt + 1) * 1024],
                                    x_in[s_idx, tt * 128:(tt + 1) * 128, :],
                                    waits=[], stream=f"x{tt}")
        else:
            mk_xt = xmarks
        mk_wo = sch.dma("gpsimd", wo_sb[:].rearrange("p (c j) -> p c j", c=8),
                        wout.rearrange("(c p) j -> p c j", p=128), stream="wo")

        if s_idx == 0:
            bvb = ab("bvb", [128, 8], BF16)
            lb0 = sch.dma("sync", recf[0:1, 0:512], bout[0:512].rearrange("(o j) -> o j", o=1), stream="cst")
            lb1 = sch.dma("sync", bcs[0:1, 0:512], bout[512:1024].rearrange("(o j) -> o j", o=1), stream="cst")
            m1 = vop(lambda e: e.tensor_copy(out=bvb[:], in_=bqkvT[:, 16:24]),
                     waits=[bk_mark])
            last = None
            for c in range(8):
                last = top(lambda e, c=c: e.matmul(
                    ps[0][0:1, :], bvb[:, c:c + 1], wo_sb[:, c * 1024:c * 1024 + 512],
                    start=(c == 0), stop=(c == 7)),
                    waits=[m1, mk_wo] if c == 0 else [])
            for c in range(8):
                last = top(lambda e, c=c: e.matmul(
                    ps[1][0:1, :], bvb[:, c:c + 1], wo_sb[:, c * 1024 + 512:(c + 1) * 1024],
                    start=(c == 0), stop=(c == 7)), waits=[])
            e1 = vop(lambda e: e.tensor_tensor(out=recf[0:1, :], in0=ps[0][0:1, :],
                                               in1=recf[0:1, :], op=ALU.add),
                     waits=[last, lb0, lb1])
            e2 = vop(lambda e: e.tensor_tensor(out=bcs[0:1, :], in0=ps[1][0:1, :],
                                               in1=bcs[0:1, :], op=ALU.add), waits=[])
            b1 = top(lambda e: e.matmul(ps[2][:], onesr_f[:], recf[0:1, :],
                                        start=True, stop=True), waits=[e1, e2])
            b2 = top(lambda e: e.matmul(ps[3][:], onesr_f[:], bcs[0:1, :],
                                        start=True, stop=True), waits=[])
            vop(lambda e: e.tensor_copy(out=beff[:, 0:512], in_=ps[2][:]), waits=[b1, b2])
            vop(lambda e: e.tensor_copy(out=beff[:, 512:1024], in_=ps[3][:]), waits=[])
            sch.barrier()

        ln1_marks = {}
        for wi in range(NW):
            for j in range(4):
                tt = wi * 4 + j
                xt = x_sb[:, tt * 1024:(tt + 1) * 1024]
                vop(lambda e, xt=xt, tt=tt: e.tensor_reduce(
                    out=st_sx[:, tt:tt + 1], in_=xt, axis=AX.X, op=ALU.add),
                    waits=[mk_xt[tt]])
                aop(lambda e, xt=xt, tt=tt: e.activation(
                    out=sqj[:], in_=xt, func=AF.Square, accum_out=st_sq[:, tt:tt + 1]),
                    waits=[mk_xt[tt]])
            ln1_marks[wi] = ln_chain(st_sx, st_sq, st_m, st_r, st_a, st_b,
                                     slice(wi * 4, wi * 4 + 4))

        x1_marks = {}
        vt_marks = {}
        kt_marks = {}
        ln2q_marks = {}
        state = {"tr_free": None, "hln_rd": [None]}
        bank_rd = [None] * 7
        slice_rd = [None, None, None, None]
        vslice_rd = [None, None]
        pt_free = [None, None]
        bcs_rd = None
        recb_rd = None

        def emit_hlnT(wi):
            last = None
            for j in range(4):
                tt = wi * 4 + j
                hb = hln[0]
                ap_m = vop(lambda e, tt=tt, hb=hb: e.tensor_scalar(
                    out=hb[:], in0=x_sb[:, tt * 1024:(tt + 1) * 1024],
                    scalar1=st_a[:, tt:tt + 1], scalar2=st_b[:, tt:tt + 1],
                    op0=ALU.mult, op1=ALU.add),
                    waits=[ln1_marks[wi], state["hln_rd"][0]])
                trs = None
                for c in range(8):
                    trs = top(lambda e, hb=hb, c=c: e.transpose(
                        tr_ps[:, c * 128:(c + 1) * 128], hb[:, c * 128:(c + 1) * 128],
                        ident_bf[:]), waits=[ap_m, state["tr_free"]] if c == 0 else [])
                state["hln_rd"][0] = trs
                last = vop(lambda e, j=j: e.tensor_copy(
                    out=hlnT[:].rearrange("p (c t) -> p c t", c=8)[:, :, j * 128:(j + 1) * 128],
                    in_=tr_ps[:].rearrange("p (c t) -> p c t", c=8)), waits=[trs])
                state["tr_free"] = last
            return last


        def emit_ln2_group(gi):
            for j in range(4):
                tt = gi * 4 + j
                xt = x_sb[:, tt * 1024:(tt + 1) * 1024]
                vop(lambda e, xt=xt, tt=tt: e.tensor_reduce(
                    out=st_sx[:, tt:tt + 1], in_=xt, axis=AX.X, op=ALU.add),
                    waits=[x1_marks[tt]])
                vop(lambda e, xt=xt, tt=tt: e.tensor_reduce(
                    out=st_mx[:, tt:tt + 1], in_=xt, axis=AX.X, op=ALU.max), waits=[])
                vop(lambda e, xt=xt, tt=tt: e.tensor_reduce(
                    out=st_mn[:, tt:tt + 1], in_=xt, axis=AX.X, op=ALU.min), waits=[])
                aop(lambda e, xt=xt, tt=tt: e.activation(
                    out=sqj_g[:], in_=xt, func=AF.Square, accum_out=st_sq[:, tt:tt + 1]),
                    waits=[x1_marks[tt]])
            sl = slice(gi * 4, gi * 4 + 4)
            ln2 = ln_chain(st_sx, st_sq, st_m, st_r, st_a, st_b, sl)
            vop(lambda e, sl=sl: e.tensor_tensor(out=st_mx[:, sl], in0=st_mx[:, sl],
                                          in1=st_m[:, sl], op=ALU.subtract), waits=[ln2])
            vop(lambda e, sl=sl: e.tensor_tensor(out=st_mn[:, sl], in0=st_m[:, sl],
                                          in1=st_mn[:, sl], op=ALU.subtract))
            vop(lambda e, sl=sl: e.tensor_tensor(out=st_mx[:, sl], in0=st_mx[:, sl],
                                          in1=st_mn[:, sl], op=ALU.max))
            vop(lambda e, sl=sl: e.tensor_tensor(out=st_mx[:, sl], in0=st_mx[:, sl],
                                          in1=st_r[:, sl], op=ALU.mult))
            mxc = vop(lambda e, sl=sl: e.tensor_single_scalar(out=st_mx[:, sl], in_=st_mx[:, sl],
                                                       scalar=1e-5, op=ALU.max))
            vop(lambda e, sl=sl: e.reciprocal(out=st_c1[:, sl], in_=st_mx[:, sl]), waits=[mxc])
            sxm = vop(lambda e, sl=sl: e.tensor_single_scalar(out=st_c1[:, sl], in_=st_c1[:, sl],
                                                       scalar=127.0, op=ALU.mult))
            vop(lambda e, sl=sl: e.tensor_tensor(out=st_a[:, sl], in0=st_r[:, sl],
                                          in1=st_c1[:, sl], op=ALU.mult), waits=[sxm])
            vop(lambda e, sl=sl: e.tensor_tensor(out=st_b[:, sl], in0=st_m[:, sl],
                                          in1=st_a[:, sl], op=ALU.mult))
            vop(lambda e, sl=sl: e.tensor_scalar(out=st_b[:, sl], in0=st_b[:, sl],
                                          scalar1=cst[:, 5:6], scalar2=cst[:, 0:1],
                                          op0=ALU.mult, op1=ALU.add), waits=[m_magic])
            vop(lambda e, sl=sl: e.tensor_single_scalar(out=st_mx[:, sl], in_=st_mx[:, sl],
                                                 scalar=1.0 / 127.0, op=ALU.mult))
            vop(lambda e, sl=sl: e.tensor_single_scalar(out=st_c1[:, sl], in_=st_mx[:, sl],
                                                 scalar=mcb[:, 0:1], op=ALU.mult))
            ln2q_marks[gi] = vop(lambda e, sl=sl: e.tensor_single_scalar(
                out=st_c2[:, sl], in_=st_mx[:, sl], scalar=mcb[:, 1:2], op=ALU.mult))

        hT_next = emit_hlnT(0)
        for w in range(NW):
            qs = w * 512
            hT_ready = hT_next

            q_marks = {}
            for f in range(16):
                buf = wsl[f % 4]
                src = wqkv_bf[:, f * 128:(f + 1) * 128].rearrange("(c p) j -> p c j", p=128)
                ldm = sch.dma("sync", buf[:].rearrange("p (c j) -> p c j", c=8), src,
                              waits=[slice_rd[f % 4]], stream=f"wsl{f % 4}")
                bank = ps[f % 4]
                mml = None
                for c in range(8):
                    mml = top(lambda e, bank=bank, buf=buf, c=c: e.matmul(
                        bank[:], buf[:, c * 128:(c + 1) * 128],
                        hlnT[:, c * 512:(c + 1) * 512],
                        start=(c == 0), stop=(c == 7)),
                        waits=[ldm, hT_ready, bank_rd[f % 4]] if c == 0 else [])
                slice_rd[f % 4] = mml
                if f < 8:
                    em = vop(lambda e, bank=bank, f=f: e.tensor_single_scalar(
                        out=QTw[:, f * 512:(f + 1) * 512], in_=bank[:],
                        scalar=bqkvT[:, f:f + 1], op=ALU.add), waits=[mml, bk_mark])
                    q_marks[f] = em
                else:
                    em = vop(lambda e, bank=bank, f=f, qs=qs: e.tensor_scalar(
                        out=KT[:, (f - 8) * S + qs:(f - 8) * S + qs + 512], in0=bank[:],
                        scalar1=cst[:, 1:2], scalar2=bqkvT[:, f:f + 1],
                        op0=ALU.mult, op1=ALU.add), waits=[mml, bk_mark])
                    kt_marks[(f - 8, w)] = em
                bank_rd[f % 4] = em

            for h in range(2):
                buf = wvs[h]
                src = wqkv_bf[:, 2048 + h * 512:2048 + (h + 1) * 512].rearrange(
                    "(c p) j -> p c j", p=128)
                ldm = sch.dma("sync", buf[:].rearrange("p (c j) -> p c j", c=8), src,
                              waits=[vslice_rd[0], vslice_rd[1]], stream="wvs")
                for j in range(4):
                    tt = w * 4 + j
                    bi = 2 + (j % 2)
                    bank = ps[bi]
                    mml = None
                    for c in range(8):
                        mml = top(lambda e, bank=bank, buf=buf, c=c, j=j: e.matmul(
                            bank[:], hlnT[:, c * 512 + j * 128:c * 512 + (j + 1) * 128],
                            buf[:, c * 512:(c + 1) * 512],
                            start=(c == 0), stop=(c == 7)),
                            waits=[ldm, hT_ready, bank_rd[bi]] if c == 0 else [])
                    em = aop(lambda e, bank=bank, tt=tt, h=h: e.activation(
                        out=VT[:, tt * 1024 + h * 512:tt * 1024 + (h + 1) * 512],
                        in_=bank[:], func=AF.Copy), waits=[mml])
                    vt_marks[(tt, h)] = em
                    bank_rd[bi] = em
                    vslice_rd[h] = mml

            if w + 1 < NW:
                hT_next = emit_hlnT(w + 1)
            if w > 0:
                emit_ln2_group(w - 1)

            if s_idx == 0 and w == 0:
                tap("d_QTw", QTw[:], waits=[q_marks[p_] for p_ in range(8)])
                tap("d_sq", st_sq[:], waits=[ln1_marks[NW - 1]])
                tap("d_sta", st_a[:], waits=[ln1_marks[NW - 1]])
            nkb = 4 * w + 4
            for p in range(8):
                exp_marks = {}
                pend = []
                vp = vps[p % 2]
                vbi = 4 + (p % 2)
                z1 = vop(lambda e, vp=vp: e.memset(vp[64:128, :], 0.0),
                         waits=[bank_rd[vbi]])
                z2 = vop(lambda e: e.memset(sg[32:33, :], 0.0),
                         waits=[bank_rd[6]])
                bank_rd[vbi] = z1
                bank_rd[6] = z2

                def emit_sv(kprev, offp):
                    pts = PT[kprev % 2]
                    st0 = (kprev == 0)
                    sp0_ = (kprev == nkb - 1)
                    top(lambda e, o=offp, b=pts, st=st0, sp=sp0_: e.matmul(
                        sg[0:1, o:512], ones_col[:], b[:, o:512],
                        start=st, stop=sp),
                        waits=[exp_marks[kprev]] +
                              ([bank_rd[6]] if kprev == 0 else []))
                    top(lambda e, o=offp, b=pts: e.matmul(
                        sg[32:33, o:512], ones_col[:], b[:, 512 + o:1024],
                        start=False, stop=False, skip_group_check=True), waits=[])
                    top(lambda e, o=offp, p=p, k=kprev, b=pts, st=st0, sp=sp0_, vp=vp: e.matmul(
                        vp[0:64, o:512],
                        VT[:, k * 1024 + 2 * p * 64:k * 1024 + (2 * p + 1) * 64],
                        b[:, o:512], start=st, stop=sp,
                        tile_position=(0, 0)),
                        waits=[vt_marks[(kprev, (2 * p * 64) // 512)],
                               vt_marks[(kprev, ((2 * p + 1) * 64) // 512)]] +
                              ([bank_rd[vbi]] if kprev == 0 else []))
                    vm_ = top(lambda e, o=offp, p=p, k=kprev, b=pts, vp=vp: e.matmul(
                        vp[64:128, o:512],
                        VT[:, k * 1024 + (2 * p + 1) * 64:k * 1024 + (2 * p + 2) * 64],
                        b[:, 512 + o:1024], start=False, stop=False,
                        tile_position=(0, 64), skip_group_check=True), waits=[])
                    pt_free[kprev % 2] = vm_
                    return vm_

                for kb in range(nkb):
                    off = max(0, kb * 128 - qs)
                    sb_t = sp[kb % 2]
                    straddle = kb * 128 >= qs
                    deps = [bank_rd[2 * (kb % 2)], bank_rd[2 * (kb % 2) + 1],
                            kt_marks[(p, kb // 4)], q_marks[p], hT_ready]
                    first = True
                    if straddle:
                        top(lambda e, b=sb_t, off=off: e.matmul(
                            b[:, off:off + 128], ident_bf[:], tri_bf[:],
                            start=True, stop=False), waits=deps)
                        top(lambda e, b=sb_t, off=off: e.matmul(
                            b[:, 512 + off:512 + off + 128], ident_bf[:], tri_bf[:],
                            start=True, stop=False), waits=[])
                        first = False
                        spans = [(off, off + 128, False)]
                        if off + 128 < 512:
                            spans.append((off + 128, 512, True))
                    else:
                        spans = [(off, 512, True)]
                    sc_m = None
                    for (o0, o1, st_flag) in spans:
                        top(lambda e, b=sb_t, o0=o0, o1=o1, p=p, kb=kb, st=st_flag: e.matmul(
                            b[:, o0:o1],
                            KT[0:64, p * S + kb * 128:p * S + (kb + 1) * 128],
                            QTw[0:64, p * 512 + o0:p * 512 + o1],
                            start=st, stop=True, tile_position=(0, 0)),
                            waits=deps if first else [])
                        first = False
                        sc_m = top(lambda e, b=sb_t, o0=o0, o1=o1, p=p, kb=kb, st=st_flag: e.matmul(
                            b[:, 512 + o0:512 + o1],
                            KT[64:128, p * S + kb * 128:p * S + (kb + 1) * 128],
                            QTw[64:128, p * 512 + o0:p * 512 + o1],
                            start=st, stop=True, tile_position=(64, 0)), waits=[])

                    if len(pend) == 2:
                        emit_sv(*pend.pop(0))
                    ew = [sc_m, pt_free[kb % 2]]
                    e1 = aop(lambda e, b=sb_t, off=off, o_=PT[kb % 2]: e.activation(
                        out=o_[:].rearrange("p (h q) -> p h q", h=2)[:, :, off:512],
                        in_=b[:].rearrange("p (h q) -> p h q", h=2)[:, :, off:512],
                        func=AF.Exp), waits=ew)
                    exp_marks[kb] = e1
                    bank_rd[2 * (kb % 2)] = e1
                    bank_rd[2 * (kb % 2) + 1] = e1
                    pend.append((kb, off))
                vals_mm = None
                while pend:
                    vals_mm = emit_sv(*pend.pop(0))

                l1 = aop(lambda e: e.activation(out=recf[0:1, :], in_=sg[0:1, :],
                                                func=AF.Ln), waits=[sch.snap("pe")])
                l2 = aop(lambda e: e.activation(out=recf[32:33, :], in_=sg[32:33, :],
                                                func=AF.Ln), waits=[])
                c1 = aop(lambda e: e.activation(out=recb[0:1, 0:512], in_=recf[0:1, :],
                                                func=AF.Exp, scale=-1.0),
                         waits=[recb_rd, l1])
                c2 = aop(lambda e: e.activation(out=recb[0:1, 512:1024], in_=recf[32:33, :],
                                                func=AF.Exp, scale=-1.0), waits=[l2])
                bm1 = top(lambda e: e.matmul(sg[0:64, :], onesr_bf[0:1, 0:64],
                                             recb[0:1, 0:512], start=True, stop=True,
                                             tile_position=(0, 0)), waits=[c1, c2])
                bm2 = top(lambda e: e.matmul(sg[64:128, :], onesr_bf[0:1, 0:64],
                                             recb[0:1, 512:1024], start=True, stop=True,
                                             tile_position=(0, 64)), waits=[])
                recb_rd = bm2
                be = aop(lambda e: e.activation(out=bcs[:], in_=sg[:], func=AF.Copy),
                         waits=[bm2, bcs_rd])
                bank_rd[6] = be
                vm = vop(lambda e, p=p, vp=vp: e.tensor_tensor(
                    out=valsT[:, p * 512:(p + 1) * 512], in0=vp[:], in1=bcs[:],
                    op=ALU.mult), waits=[vals_mm, be])
                bank_rd[vbi] = vm
                bcs_rd = vm
                if s_idx == 0 and w == 0 and p == 7:
                    tap("d_valsT", valsT[:], waits=[vm])
                    tap("d_recb", recb[:], waits=[vm])

            for jp in range(2):
                for jj in range(2):
                    j = jp * 2 + jj
                    tt = w * 4 + j
                    b0i = 0 + jj * 4
                    b1i = 1 + jj * 4
                    mml = None
                    for c in range(8):
                        mml = top(lambda e, c=c, j=j, b=ps[b0i]: e.matmul(
                            b[:], valsT[:, c * 512 + j * 128:c * 512 + (j + 1) * 128],
                            wo_sb[:, c * 1024:c * 1024 + 512],
                            start=(c == 0), stop=(c == 7)),
                            waits=[bank_rd[4], bank_rd[5], bank_rd[b0i], mk_wo] if c == 0 else [])
                    for c in range(8):
                        mml = top(lambda e, c=c, j=j, b=ps[b1i]: e.matmul(
                            b[:], valsT[:, c * 512 + j * 128:c * 512 + (j + 1) * 128],
                            wo_sb[:, c * 1024 + 512:(c + 1) * 1024],
                            start=(c == 0), stop=(c == 7)),
                            waits=[bank_rd[b1i]] if c == 0 else [])
                    ev0 = vop(lambda e, jp=jp, jj=jj, b=ps[b0i]: e.tensor_tensor(
                        out=attn[:, (jp * 2 + jj) * 1024:(jp * 2 + jj) * 1024 + 512], in0=b[:],
                        in1=beff[:, 0:512], op=ALU.add), waits=[mml])
                    ev1 = vop(lambda e, jp=jp, jj=jj, b=ps[b1i]: e.tensor_tensor(
                        out=attn[:, (jp * 2 + jj) * 1024 + 512:(jp * 2 + jj + 1) * 1024], in0=b[:],
                        in1=beff[:, 512:1024], op=ALU.add), waits=[])
                    bank_rd[b0i] = ev0
                    bank_rd[b1i] = ev1
                    vop(lambda e, jj=jj, jp=jp, w=w: e.tensor_reduce(
                        out=st_w[:, w * 4 + jp * 2 + jj:w * 4 + jp * 2 + jj + 1],
                        in_=attn[:, (jp * 2 + jj) * 1024:(jp * 2 + jj + 1) * 1024],
                        axis=AX.X, op=ALU.add), waits=[ev0, ev1])
                    aop(lambda e, jj=jj, jp=jp, w=w: e.activation(
                        out=sqj[:], in_=attn[:, (jp * 2 + jj) * 1024:(jp * 2 + jj + 1) * 1024],
                        func=AF.Square, accum_out=st_w2[:, w * 4 + jp * 2 + jj:w * 4 + jp * 2 + jj + 1]),
                        waits=[ev0, ev1])
            ln3 = ln_chain(st_w, st_w2, st_wm, st_wr, st_wa, st_wb,
                           slice(w * 4, w * 4 + 4))
            for j in range(4):
                tt = w * 4 + j
                t1 = vop(lambda e, j=j, w=w: e.tensor_scalar(
                    out=attn[:, j * 1024:(j + 1) * 1024],
                    in0=attn[:, j * 1024:(j + 1) * 1024],
                    scalar1=st_wa[:, w * 4 + j:w * 4 + j + 1],
                    scalar2=st_wb[:, w * 4 + j:w * 4 + j + 1],
                    op0=ALU.mult, op1=ALU.add), waits=[ln3])
                x1_marks[tt] = vop(lambda e, j=j, tt=tt: e.tensor_tensor(
                    out=x_sb[:, tt * 1024:(tt + 1) * 1024],
                    in0=attn[:, j * 1024:(j + 1) * 1024],
                    in1=x_sb[:, tt * 1024:(tt + 1) * 1024], op=ALU.add), waits=[t1])
            if w == NW - 1:
                emit_ln2_group(w)
            if s_idx == 0 and w == NW - 1:
                tap("d_KT", KT[:], waits=[kt_marks[(p_, NW - 1)] for p_ in range(8)])
                tap("d_VT", VT[:], waits=[vt_marks[(NT - 1, 0)], vt_marks[(NT - 1, 1)]])
                tap("d_x1", x_sb[:], waits=[x1_marks[tt_] for tt_ in range(NT)])
        sch.barrier()
        esA.close()

        esB = ExitStack()
        bb = lambda n, s_, d=F32: esB.enter_context(nc.sbuf_tensor(f"s{s_idx}b_{n}", s_, d))
        y_bf = bb("y_bf", [128, 4 * 4096], BF16)
        wbuf = bb("wbuf", [128, 16384], BF16)
        yqT = bb("yqT", [128, 16384], BF16)
        q1b = bb("q1b", [128, 1024], F32)
        q1c = bb("q1c", [128, 1024], F32)
        yq_t = bb("yq_t", [128, 1024], BF16)
        yq_c = bb("yq_c", [128, 1024], BF16)
        h2q = [bb("h2q0", [128, 1024], BF16)] * 2
        h2qT = bb("h2qT", [128, 8 * 512], BF16)
        silb0 = bb("silb0", [128, 512], F32)
        silb = [silb0, silb0]
        o_sb = bb("o_sb", [128, 4 * 1024], F32)
        sqj2 = sqj_g

        tr_free = None
        trB_free = None
        h2q_rd = [None, None]
        wbuf_rd = [None, None]
        bank_rd = [None] * 7
        sil_rd = [None]
        yq_rd = [None, None]
        ost_marks = {}
        def emit_h2q(g, gu_done):
            nonlocal tr_free
            last = None
            for j in range(4):
                tt = g * 4 + j
                hb = h2q[j % 2]
                a1 = vop(lambda e, tt=tt, g=g: e.tensor_scalar(
                    out=q1b[:], in0=x_sb[:, tt * 1024:(tt + 1) * 1024],
                    scalar1=st_a[:, tt:tt + 1], scalar2=st_b[:, tt:tt + 1],
                    op0=ALU.mult, op1=ALU.add), waits=[h2q_rd[j % 2], ln2q_marks[g]])
                a2m = vop(lambda e, hb=hb: e.tensor_single_scalar(
                    out=hb[:], in_=q1b[:], scalar=cst[:, 0:1],
                    op=ALU.subtract), waits=[a1])
                trs = None
                for c in range(8):
                    trs = top(lambda e, hb=hb, c=c: e.transpose(
                        tr_ps[:, c * 128:(c + 1) * 128], hb[:, c * 128:(c + 1) * 128],
                        ident_bf[:]), waits=[a2m, tr_free] if c == 0 else [])
                h2q_rd[j % 2] = trs
                last = vop(lambda e, j=j: e.tensor_copy(
                    out=h2qT[:].rearrange("p (c t) -> p c t", c=8)[:, :, j * 128:(j + 1) * 128],
                    in_=tr_ps[:].rearrange("p (c t) -> p c t", c=8)),
                    waits=[trs] + gu_done)
                tr_free = last
            return last

        hT_pend = None
        for g in range(4):
            if hT_pend is None:
                h2qT_ready = emit_h2q(g, [])
            else:
                h2qT_ready = hT_pend

            y_marks = []
            red_marks = {}
            for iw in range(8):
                reg = iw % 2
                gsl = wbuf[:, reg * 8192:reg * 8192 + 4096]
                usl = wbuf[:, reg * 8192 + 4096:reg * 8192 + 8192]
                lg = sch.dma("sync", gsl.rearrange("p (c j) -> p c j", c=8),
                             wgq[:, iw * 512:(iw + 1) * 512].rearrange(
                                 "(c p) j -> p c j", p=128), waits=[wbuf_rd[reg]],
                             stream=f"gu{reg}")
                lu = sch.dma("sync", usl.rearrange("p (c j) -> p c j", c=8),
                             wuq[:, iw * 512:(iw + 1) * 512].rearrange(
                                 "(c p) j -> p c j", p=128), waits=[],
                             stream=f"gu{reg}")
                for j in range(4):
                    tt = g * 4 + j
                    par = j % 2
                    bg, bu = pb[par * 2], pb[par * 2 + 1]
                    sb_ = silb[par]
                    mml = None
                    for c in range(8):
                        mml = top(lambda e, bg=bg, gsl=gsl, c=c, j=j: e.matmul(
                            bg[:], h2qT[:, c * 512 + j * 128:c * 512 + (j + 1) * 128],
                            gsl[:, c * 512:(c + 1) * 512],
                            start=(c == 0), stop=(c == 7)),
                            waits=[lg, lu, h2qT_ready, bank_rd[par * 2]] if c == 0 else [])
                    for c in range(8):
                        mml = top(lambda e, bu=bu, usl=usl, c=c, j=j: e.matmul(
                            bu[:], h2qT[:, c * 512 + j * 128:c * 512 + (j + 1) * 128],
                            usl[:, c * 512:(c + 1) * 512],
                            start=(c == 0), stop=(c == 7)),
                            waits=[bank_rd[par * 2 + 1]] if c == 0 else [])
                    wbuf_rd[reg] = mml
                    sm = aop(lambda e, bg=bg, tt=tt, sb_=sb_: e.activation(
                        out=sb_[:], in_=bg[:], func=AF.Silu,
                        scale=st_c1[:, tt:tt + 1]), waits=[mml, sil_rd[0], ln2q_marks[g]])
                    ym = vop(lambda e, bu=bu, tt=tt, j=j, iw=iw, sb_=sb_: e.scalar_tensor_tensor(
                        out=y_bf[:, j * 4096 + iw * 512:j * 4096 + (iw + 1) * 512],
                        in0=bu[:], scalar=st_c2[:, tt:tt + 1], in1=sb_[:],
                        op0=ALU.mult, op1=ALU.mult), waits=[sm])
                    rm = vop(lambda e, j=j, iw=iw: e.tensor_reduce(
                        out=st_p[:, j * 8 + iw:j * 8 + iw + 1],
                        in_=y_bf[:, j * 4096 + iw * 512:j * 4096 + (iw + 1) * 512],
                        axis=AX.X, op=ALU.max, apply_absolute_value=True), waits=[ym])
                    red_marks[(j, iw)] = rm
                    sil_rd[0] = ym
                    bank_rd[par * 2] = sm
                    bank_rd[par * 2 + 1] = ym
                    y_marks.append(ym)
            y_done = y_marks[-1]
            if s_idx == 0 and g == 0:
                tap("d_h2qT", h2qT[:], waits=[h2qT_ready])
                tap("d_y", y_bf[:], waits=[y_done])

            cd_marks = {}
            sy_marks = {}
            for j in range(4):
                tt = g * 4 + j
                c0m = vop(lambda e, j=j, tt=tt: e.tensor_reduce(
                    out=st_my[:, tt:tt + 1], in_=st_p[:, j * 8:(j + 1) * 8],
                    axis=AX.X, op=ALU.max), waits=[red_marks[(j, 7)]])
                c1m = vop(lambda e, tt=tt: e.tensor_single_scalar(
                    out=st_my[:, tt:tt + 1], in_=st_my[:, tt:tt + 1],
                    scalar=1e-5, op=ALU.max))
                c2m = vop(lambda e, tt=tt: e.reciprocal(
                    out=st_cd[:, tt:tt + 1], in_=st_my[:, tt:tt + 1]), waits=[c1m])
                sy_marks[j] = vop(lambda e, tt=tt: e.tensor_single_scalar(
                    out=st_cd[:, tt:tt + 1], in_=st_cd[:, tt:tt + 1],
                    scalar=127.0, op=ALU.mult), waits=[c2m])
                vop(lambda e, tt=tt: e.tensor_single_scalar(
                    out=st_my[:, tt:tt + 1], in_=st_my[:, tt:tt + 1],
                    scalar=1.0 / 127.0, op=ALU.mult), waits=[sy_marks[j]])
                cd_marks[j] = vop(lambda e, tt=tt: e.tensor_single_scalar(
                    out=st_my[:, tt:tt + 1], in_=st_my[:, tt:tt + 1],
                    scalar=mcb[:, 2:3], op=ALU.mult))
            cdm = cd_marks[3]

            def gop(fn, waits=(), inc="gp"):
                return sch.op("gpsimd", fn, waits=waits, inc=inc)

            tq_marks = {}
            dn_ev = {}
            wd_rd = None
            lw0 = sch.dma("sync", wbuf[:].rearrange("p (c j) -> p c j", c=32),
                          wdq[:, 0:512].rearrange("(c p) j -> p c j", p=128),
                          waits=[wbuf_rd[0], wbuf_rd[1]], stream="wd")

            trB = sp[0][:].bitcast(BF16)

            def emit_rt(qt):
                nonlocal yq_rd, tr_free, trB_free
                r1 = {}
                rl = {}
                tr = {}
                cp = {}

                def mk_r1(j, extra):
                    tt = g * 4 + j
                    qb = q1b if (qt + j) % 2 == 0 else q1c
                    return aop(lambda e, j=j, qt=qt, tt=tt, qb=qb: e.activation(
                        out=qb[:],
                        in_=y_bf[:, j * 4096 + qt * 1024:j * 4096 + (qt + 1) * 1024],
                        func=AF.Copy, scale=st_cd[:, tt:tt + 1]),
                        waits=[sy_marks[j], extra])

                def mk_rl(j, extra):
                    qb = q1b if (qt + j) % 2 == 0 else q1c
                    yqt = yq_t if (qt + j) % 2 == 0 else yq_c
                    return vop(lambda e, qb=qb, yqt=yqt: e.tensor_scalar(
                        out=yqt[:], in0=qb[:],
                        scalar1=cst[:, 0:1], scalar2=cst[:, 0:1],
                        op0=ALU.add, op1=ALU.subtract),
                        waits=[r1[j], extra])

                def mk_tr(j):
                    yqt = yq_t if (qt + j) % 2 == 0 else yq_c
                    tgt = tr_ps if j % 2 == 0 else trB
                    free = tr_free if j % 2 == 0 else trB_free
                    trs = None
                    for c in range(8):
                        trs = top(lambda e, c=c, yqt=yqt, tgt=tgt: e.transpose(
                            tgt[:, c * 128:(c + 1) * 128],
                            yqt[:, c * 128:(c + 1) * 128], ident_bf[:]),
                            waits=[rl[j], free, bank_rd[0] if j % 2 else None]
                            if c == 0 else [])
                    return trs

                def mk_cp(j):
                    tgt = tr_ps if j % 2 == 0 else trB
                    return vop(lambda e, j=j, qt=qt, tgt=tgt: e.tensor_copy(
                        out=yqT[:].rearrange("p (c t) -> p c t", c=32)
                            [:, qt * 8:(qt + 1) * 8, j * 128:(j + 1) * 128],
                        in_=tgt[:, 0:1024].rearrange("p (c t) -> p c t", c=8)),
                        waits=[tr[j]])

                r1[0] = mk_r1(0, yq_rd[qt % 2])
                r1[1] = mk_r1(1, yq_rd[(qt + 1) % 2])
                rl[0] = mk_rl(0, yq_rd[qt % 2])
                rl[1] = mk_rl(1, yq_rd[(qt + 1) % 2])
                tr[0] = mk_tr(0)
                yq_rd[qt % 2] = tr[0]
                r1[2] = mk_r1(2, rl[0])
                rl[2] = mk_rl(2, tr[0])
                cp[0] = mk_cp(0)
                tr_free = cp[0]
                tr[1] = mk_tr(1)
                yq_rd[(qt + 1) % 2] = tr[1]
                r1[3] = mk_r1(3, rl[1])
                rl[3] = mk_rl(3, tr[1])
                cp[1] = mk_cp(1)
                trB_free = cp[1]
                tr[2] = mk_tr(2)
                yq_rd[qt % 2] = tr[2]
                cp[2] = mk_cp(2)
                tr_free = cp[2]
                tr[3] = mk_tr(3)
                yq_rd[(qt + 1) % 2] = tr[3]
                cp[3] = mk_cp(3)
                trB_free = cp[3]
                bank_rd[0] = cp[3]
                return cp[3]

            def emit_down_chunk(hw, tp, ic0, ic1, lw):
                mml = None
                for ic in range(ic0, ic1):
                    for t2 in range(2):
                        j = tp * 2 + t2
                        bank = pb[2 + t2]
                        mmw = []
                        if ic == 0:
                            mmw = [lw, tq_marks[0], bank_rd[2 + t2],
                                   dn_ev.get((hw - 1, j)), dn_ev.get((hw, j - 2))]
                        elif ic % 8 == 0:
                            mmw = [tq_marks[ic // 8]]
                        mml = top(lambda e, ic=ic, j=j, bank=bank: e.matmul(
                            bank[:], yqT[:, ic * 512 + j * 128:ic * 512 + (j + 1) * 128],
                            wbuf[:, ic * 512:(ic + 1) * 512],
                            start=(ic == 0), stop=(ic == 31)), waits=mmw)
                return mml

            def emit_down_evac(hw, tp):
                pe_done = sch.snap("pe")
                for t2 in range(2):
                    j = tp * 2 + t2
                    tt = g * 4 + j
                    dn_ev[(hw, j)] = vop(lambda e, j=j, hw=hw, tt=tt, t2=t2: e.tensor_single_scalar(
                        out=o_sb[:, j * 1024 + hw * 512:j * 1024 + (hw + 1) * 512],
                        in_=pb[2 + t2][:], scalar=st_my[:, tt:tt + 1], op=ALU.mult),
                        waits=[pe_done, cd_marks[j], ost_marks.get(j)])
                    bank_rd[2 + t2] = dn_ev[(hw, j)]

            for qt in range(4):
                tq_marks[qt] = emit_rt(qt)
                wd_rd = emit_down_chunk(0, 0, qt * 8, (qt + 1) * 8, lw0)
            emit_down_evac(0, 0)
            wd_rd = emit_down_chunk(0, 1, 0, 32, lw0)
            emit_down_evac(0, 1)
            lw1 = sch.dma("sync", wbuf[:].rearrange("p (c j) -> p c j", c=32),
                          wdq[:, 512:1024].rearrange("(c p) j -> p c j", p=128),
                          waits=[wd_rd], stream="wd")
            wd_rd = emit_down_chunk(1, 0, 0, 32, lw1)
            if g + 1 < 4:
                hT_pend = emit_h2q(g + 1, [wbuf_rd[0], wbuf_rd[1]])
            emit_down_evac(1, 0)
            wd_rd = emit_down_chunk(1, 1, 0, 32, lw1)
            emit_down_evac(1, 1)
            wbuf_rd = [wd_rd, wd_rd]

            for j in range(4):
                tt = g * 4 + j
                w0 = [dn_ev[(0, j)], dn_ev[(1, j)]]
                vop(lambda e, tt=tt, j=j: e.tensor_reduce(
                    out=st_w[:, tt:tt + 1], in_=o_sb[:, j * 1024:(j + 1) * 1024],
                    axis=AX.X, op=ALU.add), waits=w0)
                aop(lambda e, tt=tt, j=j: e.activation(
                    out=sqj2[:], in_=o_sb[:, j * 1024:(j + 1) * 1024], func=AF.Square,
                    accum_out=st_w2[:, tt:tt + 1]), waits=w0)
            ln4 = ln_chain(st_w, st_w2, st_wm, st_wr, st_wa, st_wb,
                           slice(g * 4, g * 4 + 4))
            for j in range(4):
                tt = g * 4 + j
                t1 = vop(lambda e, tt=tt, j=j: e.tensor_scalar(
                    out=o_sb[:, j * 1024:(j + 1) * 1024],
                    in0=o_sb[:, j * 1024:(j + 1) * 1024],
                    scalar1=st_wa[:, tt:tt + 1], scalar2=st_wb[:, tt:tt + 1],
                    op0=ALU.mult, op1=ALU.add), waits=[ln4])
                om = vop(lambda e, j=j, tt=tt: e.tensor_tensor(
                    out=o_sb[:, j * 1024:(j + 1) * 1024],
                    in0=o_sb[:, j * 1024:(j + 1) * 1024],
                    in1=x_sb[:, tt * 1024:(tt + 1) * 1024], op=ALU.add), waits=[t1])
                ost_marks[j] = sch.dma("gpsimd", out_hbm[s_idx, tt * 128:(tt + 1) * 128, :],
                        o_sb[:, j * 1024:(j + 1) * 1024], waits=[om], stream=f"out{j}")
                if s_idx + 1 < NSEQ:
                    next_xmarks[tt] = sch.dma(
                        "gpsimd", x_sb[:, tt * 1024:(tt + 1) * 1024],
                        x_in[s_idx + 1, tt * 128:(tt + 1) * 128, :],
                        waits=[om], stream=f"xn{tt}")
                if s_idx == 0 and g == 0 and j == 3:
                    tap("d_osb", o_sb[:], waits=[om])
        sch.barrier()
        esB.close()
        return next_xmarks

    nxt = None
    for s_idx in range(NSEQ):
        nxt = emit_seq(s_idx, nxt)
    sch.barrier()

    with nc.Block() as block:
        sch.build(block)
    return nc


def kernel(L0_attn_x, L0_attn_W_qkv, L0_attn_b_qkv, L0_attn_W_out, L0_attn_b_out,
           L0_ffn_w_gate, L0_ffn_w_up, L0_ffn_w_down):
    x = np.ascontiguousarray(np.asarray(L0_attn_x, dtype=np.float32))
    in_common = {
        "wqkv": np.ascontiguousarray(np.asarray(L0_attn_W_qkv, np.float32)),
        "bqkv": np.ascontiguousarray(np.asarray(L0_attn_b_qkv, np.float32)),
        "wout": np.ascontiguousarray(np.asarray(L0_attn_W_out, np.float32)),
        "bout": np.ascontiguousarray(np.asarray(L0_attn_b_out, np.float32)),
        "wgT": np.ascontiguousarray(np.asarray(L0_ffn_w_gate, np.float32).T),
        "wuT": np.ascontiguousarray(np.asarray(L0_ffn_w_up, np.float32).T),
        "wdT": np.ascontiguousarray(np.asarray(L0_ffn_w_down, np.float32).T),
    }
    nc = build_program()
    in_maps = []
    for c in range(NCORES):
        m = dict(in_common)
        m["x"] = np.ascontiguousarray(x[c * NSEQ:(c + 1) * NSEQ])
        in_maps.append(m)
    res = run_bass_kernel_spmd(nc, in_maps, core_ids=list(range(NCORES)))
    return np.concatenate([r["out"] for r in res.results], axis=0)


# version 31
# speedup vs baseline: 1.0155x; 1.0105x over previous
import numpy as np
import ml_dtypes
from contextlib import ExitStack

import concourse.bass as bass
import concourse.mybir as mybir
from concourse import dve_ops as _dve_ops
from concourse.bass_utils import run_bass_kernel_spmd

BF = ml_dtypes.bfloat16
F32 = mybir.dt.float32
BF16 = mybir.dt.bfloat16
AF = mybir.ActivationFunctionType
ALU = mybir.AluOpType
AX = mybir.AxisListType

NCORES = 8
NSEQ = 2
S = 2048
D = 1024
INNER = 4096
NT = S // 128
NW = S // 512
EPS = 1e-5

ENGS = ("sync", "tensor", "scalar", "vector", "gpsimd")
_DEBUG = False


class Sched:
    def __init__(self, nc):
        self.nc = nc
        self.sems = {}
        self.cnt = {}
        self.items = {e: [] for e in ENGS}
        self.hwm = {e: {} for e in ENGS}
        self.dma_streams = {}

    def sem(self, name):
        if name not in self.sems:
            self.sems[name] = self.nc.alloc_semaphore(name)
            self.cnt[name] = 0
        return self.sems[name]

    def op(self, eng, fn, waits=(), inc=None, dk=1):
        waits = list(waits)
        if inc in ("dve", "act", "gp"):
            self.sem(inc)
            waits.append((inc, self.cnt[inc]))
        merged = {}
        for wv in waits:
            if wv is None:
                continue
            s, v = wv
            if v is None or v <= 0:
                continue
            if v > merged.get(s, 0):
                merged[s] = v
        ws = []
        for s, v in merged.items():
            self.sem(s)
            if v > self.hwm[eng].get(s, 0):
                ws.append((s, v))
                self.hwm[eng][s] = v
        mark = None
        if inc is not None:
            self.sem(inc)
            self.cnt[inc] += dk
            mark = (inc, self.cnt[inc])
        self.items[eng].append((ws, fn, inc, dk))
        return mark

    def dma(self, eng, out_ap, in_ap, waits=(), stream="misc"):
        sname = "D_" + stream
        self.dma_streams.setdefault(eng, set()).add(sname)
        return self.op(eng, lambda e, o=out_ap, i=in_ap: e.dma_start(out=o, in_=i),
                       waits=waits, inc=sname, dk=16)

    def snap(self, name):
        self.sem(name)
        return (name, self.cnt[name])

    def barrier(self):
        for e in ENGS:
            w = [(s, self.cnt[s]) for s in self.dma_streams.get(e, ())]
            self.op(e, lambda eng: eng.nop(), waits=w, inc="bar")
        tgt = self.cnt["bar"]
        for e in ENGS:
            self.op(e, None, waits=[("bar", tgt)])
        return ("bar", tgt)

    def build(self, block):
        for ename in ENGS:
            items = self.items[ename]

            def body(eng, items=items):
                for ws, fn, inc, dk in items:
                    for s, v in ws:
                        eng.wait_ge(self.sems[s], v)
                    if fn is None:
                        continue
                    inst = fn(eng)
                    if inc is not None and inst is not None:
                        inst.then_inc(self.sems[inc], dk)

            getattr(block, ename)(body)


def build_program():
    nc = bass.Bass("TRN2", target_bir_lowering=False)

    x_in = nc.dram_tensor("x", [NSEQ, S, D], F32, kind="ExternalInput").ap()
    wqkv = nc.dram_tensor("wqkv", [D, 3 * D], F32, kind="ExternalInput").ap()
    bqkv = nc.dram_tensor("bqkv", [3 * D], F32, kind="ExternalInput").ap()
    wout = nc.dram_tensor("wout", [D, D], F32, kind="ExternalInput").ap()
    bout = nc.dram_tensor("bout", [D], F32, kind="ExternalInput").ap()
    wgT = nc.dram_tensor("wgT", [D, INNER], F32, kind="ExternalInput").ap()
    wuT = nc.dram_tensor("wuT", [D, INNER], F32, kind="ExternalInput").ap()
    wdT = nc.dram_tensor("wdT", [INNER, D], F32, kind="ExternalInput").ap()
    out_hbm = nc.dram_tensor("out", [NSEQ, S, D], F32, kind="ExternalOutput").ap()
    dbg = {}
    if _DEBUG:
        for nm, shp, dt in [("d_KT", [128, 8 * S], "bf"), ("d_VT", [128, NT * 1024], "bf"),
                            ("d_QTw", [128, 8 * 512], "bf"), ("d_valsT", [128, 8 * 512], "bf"),
                            ("d_x1", [128, NT * 1024], "f"), ("d_h2qT", [128, 8 * 512], "bf"),
                            ("d_y", [128, 4 * 4096], "bf"), ("d_osb", [128, 4 * 1024], "f"),
                            ("d_wq", [128, 4096], "bf"), ("d_sq", [128, NT], "f"),
                            ("d_ffa", [128, NT], "f"), ("d_ffb", [128, NT], "f"),
                            ("d_ffc1", [128, NT], "f"), ("d_ffmx", [128, NT], "f"),
                            ("d_ffr", [128, NT], "f"), ("d_ffm", [128, NT], "f"),
                            ("d_sta", [128, NT], "f"), ("d_recb", [1, 1024], "bf")]:
            dbg[nm] = nc.dram_tensor(nm, shp, BF16 if dt == "bf" else F32,
                                     kind="ExternalOutput").ap()
    wqkv_bf = nc.dram_tensor("wqkv_bf", [D, 3 * D], BF16, kind="Internal").ap()
    wgq = nc.dram_tensor("wgq", [D, INNER], BF16, kind="Internal").ap()
    wuq = nc.dram_tensor("wuq", [D, INNER], BF16, kind="Internal").ap()
    wdq = nc.dram_tensor("wdq", [INNER, D], BF16, kind="Internal").ap()

    ident_bf_h = nc.inline_tensor(np.eye(128, dtype=BF), name="c_ident_bf")
    ident_f_h = nc.inline_tensor(np.eye(128, dtype=np.float32), name="c_ident_f")
    tri_np = np.where(np.arange(128)[:, None] > np.arange(128)[None, :],
                      np.float32(-30000.0), np.float32(0.0)).astype(BF)
    tri_h = nc.inline_tensor(tri_np, name="c_tri")

    sch = Sched(nc)
    es_root = ExitStack()
    sb = lambda n, s_, d=F32: es_root.enter_context(nc.sbuf_tensor(n, s_, d))

    ident_bf = sb("ident_bf", [128, 128], BF16)
    tri_bf = sb("tri_bf", [128, 128], BF16)
    ones_col = sb("ones_col", [128, 1], BF16)
    onesr_bf = sb("onesr_bf", [1, 128], BF16)
    onesr_f = sb("onesr_f", [1, 128], F32)
    cst = sb("cst", [128, 8], F32)
    bqkvT = sb("bqkvT", [128, 24], F32)
    beff = sb("beff", [128, 1024], BF16)
    sqj_g = sb("sqj_g", [128, 1024], BF16)
    mcb = sb("mcb", [128, 4], F32)
    thrb = sb("thrb", [128, 4], F32)
    wsums = sb("wsums", [128, 48], F32)
    wtot = sb("wtot", [1, 8], F32)
    st_sx = sb("st_sx", [128, NT], F32)
    st_sq = sb("st_sq", [128, NT], F32)
    st_mx = sb("st_mx", [128, NT], F32)
    st_mn = sb("st_mn", [128, NT], F32)
    st_m = sb("st_m", [128, NT], F32)
    st_r = sb("st_r", [128, NT], F32)
    st_a = sb("st_a", [128, NT], F32)
    st_b = sb("st_b", [128, NT], F32)
    st_c1 = sb("st_c1", [128, NT], F32)
    st_c2 = sb("st_c2", [128, NT], F32)
    st_my = sb("st_my", [128, NT], F32)
    st_p = sb("st_p", [128, 32], F32)
    st_cd = sb("st_cd", [128, NT], F32)
    st_w = sb("st_w", [128, 16], F32)
    st_w2 = sb("st_w2", [128, 16], F32)
    st_wa = sb("st_wa", [128, 16], F32)
    st_wb = sb("st_wb", [128, 16], F32)
    st_wm = sb("st_wm", [128, 16], F32)
    st_wr = sb("st_wr", [128, 16], F32)
    x_sb = sb("x_sb", [128, NT * 1024], F32)

    def tap(name, sbuf_ap, waits=()):
        if _DEBUG and name in dbg:
            return sch.dma("sync", dbg[name], sbuf_ap, waits=waits, stream="tap")
        return None

    def vop(fn, waits=(), inc="dve"):
        return sch.op("vector", fn, waits=waits, inc=inc)

    def aop(fn, waits=(), inc="act"):
        return sch.op("scalar", fn, waits=waits, inc=inc)

    def top(fn, waits=(), inc="pe"):
        return sch.op("tensor", fn, waits=waits, inc=inc)

    def ln_chain(sx, sq, m_t, r_t, a_t, b_t, s_, denom=1024.0):
        act_now = sch.snap("act")
        m1 = vop(lambda e: e.tensor_single_scalar(
            out=m_t[:, s_], in_=sx[:, s_], scalar=1.0 / denom, op=ALU.mult),
            waits=[act_now])
        vop(lambda e: e.tensor_tensor(
            out=b_t[:, s_], in0=m_t[:, s_], in1=m_t[:, s_], op=ALU.mult))
        m3a = vop(lambda e: e.scalar_tensor_tensor(
            out=r_t[:, s_], in0=sq[:, s_], scalar=1.0 / denom, in1=b_t[:, s_],
            op0=ALU.mult, op1=ALU.subtract))
        m3 = vop(lambda e: e.tensor_single_scalar(
            out=r_t[:, s_], in_=r_t[:, s_], scalar=0.0, op=ALU.max))
        a1 = aop(lambda e: e.activation(
            out=r_t[:, s_], in_=r_t[:, s_], func=AF.Ln, bias=cst[:, 2:3], scale=1.0), waits=[m3])
        a2 = aop(lambda e: e.activation(
            out=r_t[:, s_], in_=r_t[:, s_], func=AF.Exp, bias=0.0, scale=-0.5), waits=[a1])
        m4 = vop(lambda e: e.tensor_copy(out=a_t[:, s_], in_=r_t[:, s_]), waits=[a2])
        m5 = vop(lambda e: e.scalar_tensor_tensor(
            out=b_t[:, s_], in0=m_t[:, s_], scalar=-1.0, in1=a_t[:, s_],
            op0=ALU.mult, op1=ALU.mult), waits=[m4])
        return m5

    es_prep_sb = ExitStack()
    ident_f = es_prep_sb.enter_context(nc.sbuf_tensor("ident_f", [128, 128], F32))
    b24 = es_prep_sb.enter_context(nc.sbuf_tensor("b24", [24, 128], F32))
    sch.dma("sync", ident_bf[:], ident_bf_h.ap(), stream="cst")
    sch.dma("sync", ident_f[:], ident_f_h.ap(), stream="cst")
    sch.dma("sync", tri_bf[:], tri_h.ap(), stream="cst")
    sch.dma("sync", b24[:], bqkv.rearrange("(c p) -> c p", p=128), stream="cst")
    ld0 = sch.snap("D_cst")

    vop(lambda e: e.memset(ones_col[:], 1.0))
    vop(lambda e: e.memset(onesr_bf[:], 1.0))
    vop(lambda e: e.memset(onesr_f[:], 1.0))
    vop(lambda e: e.memset(cst[:, 1:2], 0.125))
    vop(lambda e: e.memset(cst[:, 2:3], EPS))
    vop(lambda e: e.memset(cst[:, 3:4], 1.0))
    vop(lambda e: e.memset(cst[:, 5:6], -1.0))
    vop(lambda e: e.memset(cst[:, 4:5], 1.0))
    vop(lambda e: e.tensor_scalar(out=cst[:, 4:5], in0=cst[:, 4:5], scalar1=2.0 ** -30,
                                  scalar2=1.0, op0=ALU.add, op1=ALU.subtract))
    m_magic = vop(lambda e: e.tensor_scalar(
        out=cst[:, 0:1], in0=cst[:, 4:5],
        scalar1=2.0 ** 30 * (1.5 * 2.0 ** 52 - 1.5 * 2.0 ** 23), scalar2=1.5 * 2.0 ** 23,
        op0=ALU.mult, op1=ALU.add))

    es_prep_ps = ExitStack()
    pbias = es_prep_ps.enter_context(nc.psum_tensor("pbias", [128, 24], F32))
    trm = top(lambda e: e.transpose(pbias[:], b24[:], ident_f[0:24, 0:24]), waits=[ld0])
    evm = vop(lambda e: e.tensor_copy(out=bqkvT[:], in_=pbias[:]), waits=[trm])
    bk_mark = vop(lambda e: e.tensor_single_scalar(
        out=bqkvT[:, 8:16], in_=bqkvT[:, 8:16], scalar=0.125, op=ALU.mult), waits=[evm])

    sch.barrier()

    prep = ExitStack()
    psb = lambda n, s_, d=F32: prep.enter_context(nc.sbuf_tensor(n, s_, d))
    wf = [psb(f"wf{i}", [128, 4096], F32) for i in range(3)]
    wjunk = psb("wjunk", [128, 4096], BF16)
    wabs = [psb(f"wabs{i}", [128, 4096], F32) for i in range(2)]
    wsgn = [psb(f"wsgn{i}", [128, 4096], BF16) for i in range(2)]
    wqb = [psb(f"wqb{i}", [128, 4096], BF16) for i in range(2)]
    onescol_f = psb("onescol_f", [128, 1], F32)
    vop(lambda e: e.memset(onescol_f[:], 1.0))

    for cb in range(6):
        sch.dma("gpsimd", wqkv_bf[:, cb * 512:(cb + 1) * 512],
                wqkv[:, cb * 512:(cb + 1) * 512], stream=f"wqc{cb}")

    mats = [(wgT, wgq, 8, 4096), (wuT, wuq, 8, 4096), (wdT, wdq, 32, 1024)]
    col = 0
    tmarks = []
    for mi, (wsrc, wdst, ntile, wid) in enumerate(mats):
        for t in range(ntile):
            buf = wf[col % 3]
            w = [tmarks[-3]] if len(tmarks) >= 3 else []
            ldm = sch.dma("sync", buf[:, 0:wid], wsrc[t * 128:(t + 1) * 128, :],
                          waits=w, stream=f"wf{col % 3}")
            am = aop(lambda e, b=buf, wd=wid, c=col: e.activation(
                out=wjunk[:, 0:wd], in_=b[:, 0:wd], func=AF.Abs,
                accum_out=wsums[:, c:c + 1]), waits=[ldm])
            tmarks.append(am)
            col += 1
    ncols = col

    ptot = es_prep_ps.enter_context(nc.psum_tensor("ptot", [1, 64], F32))
    pbc = es_prep_ps.enter_context(nc.psum_tensor("pbc", [128, 8], F32))
    tm = top(lambda e: e.matmul(ptot[0:1, 0:ncols], onescol_f[:], wsums[:, 0:ncols],
                                start=True, stop=True),
             waits=[tmarks[-1], sch.snap("dve")])
    c0 = 0
    redm = []
    for mi, (wsrc, wdst, ntile, wid) in enumerate(mats):
        rm = vop(lambda e, a=c0, b=c0 + ntile, i=mi: e.tensor_reduce(
            out=wtot[0:1, i:i + 1], in_=ptot[0:1, a:b], axis=AX.X, op=ALU.add), waits=[tm])
        redm.append(rm)
        c0 += ntile
    scm = []
    for mi, (wsrc, wdst, ntile, wid) in enumerate(mats):
        n_el = float(ntile * 128 * wid)
        scm.append(vop(lambda e, i=mi, ne=n_el: e.tensor_scalar(
            out=wtot[0:1, i:i + 1], in0=wtot[0:1, i:i + 1],
            scalar1=1.0 / ne, scalar2=1e-5, op0=ALU.mult, op1=ALU.max), waits=[redm[mi]]))
    thrm = vop(lambda e: e.tensor_single_scalar(
        out=wtot[0:1, 4:7], in_=wtot[0:1, 0:3], scalar=0.5, op=ALU.mult), waits=scm)
    bc1 = top(lambda e: e.matmul(pbc[:, 0:3], onesr_f[:], wtot[0:1, 0:3],
                                 start=True, stop=True), waits=[thrm])
    bc2 = top(lambda e: e.matmul(pbc[:, 4:7], onesr_f[:], wtot[0:1, 4:7],
                                 start=True, stop=True), waits=[thrm])
    vop(lambda e: e.tensor_copy(out=mcb[:, 0:3], in_=pbc[:, 0:3]), waits=[bc1, bc2])
    vop(lambda e: e.tensor_copy(out=thrb[:, 0:3], in_=pbc[:, 4:7]), waits=[bc1, bc2])
    sch.barrier()

    prevq = []
    last_qm = {}
    st_marks = []
    col = 0
    for mi, (wsrc, wdst, ntile, wid) in enumerate(mats):
        for t in range(ntile):
            buf = wf[col % 3]
            par = col % 2
            w = [prevq[-3]] if len(prevq) >= 3 else []
            ldm = sch.dma("sync", buf[:, 0:wid], wsrc[t * 128:(t + 1) * 128, :],
                          waits=w, stream=f"wf{col % 3}")
            a1 = aop(lambda e, b=buf, wd=wid, par=par: e.activation(
                out=wabs[par][:, 0:wd], in_=b[:, 0:wd], func=AF.Abs),
                waits=[ldm, last_qm.get(par)])
            a2 = aop(lambda e, b=buf, wd=wid, par=par: e.activation(
                out=wsgn[par][:, 0:wd], in_=b[:, 0:wd], func=AF.Sign), waits=[])
            qm = vop(lambda e, wd=wid, i=mi, par=par: e.scalar_tensor_tensor(
                out=wqb[par][:, 0:wd], in0=wabs[par][:, 0:wd], scalar=thrb[:, i:i + 1],
                in1=wsgn[par][:, 0:wd], op0=ALU.is_gt, op1=ALU.mult),
                waits=[a1, a2, st_marks[-2] if len(st_marks) >= 2 else None])
            last_qm[par] = qm
            stm = sch.dma("gpsimd", wdst[t * 128:(t + 1) * 128, :], wqb[par][:, 0:wid],
                          waits=[qm], stream=f"wst{par}")
            st_marks.append(stm)
            prevq.append(qm)
            if mi == 0 and t == 0:
                tpm = tap("d_wq", wqb[par][:], waits=[qm])
                if tpm:
                    st_marks.append(tpm)
            col += 1
    prevq = st_marks
    sch.barrier()
    prep.close()
    es_prep_sb.close()
    es_prep_ps.close()

    sp = [es_root.enter_context(nc.psum_tensor(f"sp{i}", [128, 1024], F32)) for i in range(2)]
    vps = [es_root.enter_context(nc.psum_tensor(f"vps{i}", [128, 512], F32)) for i in range(2)]
    sg = es_root.enter_context(nc.psum_tensor("sg", [128, 512], F32))
    tr_ps = es_root.enter_context(nc.psum_tensor("tr_ps", [128, 1024], BF16))
    pb = [sp[0][:, 0:512], sp[0][:, 512:1024], sp[1][:, 0:512], sp[1][:, 512:1024]]
    ps = pb + [vps[0][:], vps[1][:], sg[:]]

    def emit_seq(s_idx, xmarks=None):
        esA = ExitStack()
        ab = lambda n, s_, d=F32: esA.enter_context(nc.sbuf_tensor(f"s{s_idx}_{n}", s_, d))
        KT = ab("KT", [128, 8 * S], BF16)
        VT = ab("VT", [128, NT * 1024], BF16)
        QTw = ab("QTw", [128, 8 * 512], BF16)
        hln0 = ab("hln0", [128, 1024], BF16)
        hln = [hln0, hln0]
        hlnT = ab("hlnT", [128, 8 * 512], BF16)
        wsl = [ab(f"wsl{i}", [128, 8 * 128], BF16) for i in range(4)]
        wvs0 = ab("wvs0", [128, 8 * 512], BF16)
        wvs = [wvs0, wvs0]
        wo_sb = ab("wo_sb", [128, 8 * 1024], BF16)
        PT = [ab(f"PT{i}", [128, 1024], BF16) for i in range(2)]
        recf = ab("recf", [128, 512], F32)
        recb = ab("recb", [1, 1024], BF16)
        bcs = ab("bcs", [128, 512], F32)
        valsT = ab("valsT", [128, 8 * 512], BF16)
        attn = QTw
        sqj = sqj_g

        next_xmarks = {}
        if xmarks is None:
            mk_xt = {}
            for tt in range(NT):
                mk_xt[tt] = sch.dma("sync", x_sb[:, tt * 1024:(tt + 1) * 1024],
                                    x_in[s_idx, tt * 128:(tt + 1) * 128, :],
                                    waits=[], stream=f"x{tt}")
        else:
            mk_xt = xmarks
        mk_wo = sch.dma("gpsimd", wo_sb[:].rearrange("p (c j) -> p c j", c=8),
                        wout.rearrange("(c p) j -> p c j", p=128), stream="wo")

        if s_idx == 0:
            bvb = ab("bvb", [128, 8], BF16)
            lb0 = sch.dma("sync", recf[0:1, 0:512], bout[0:512].rearrange("(o j) -> o j", o=1), stream="cst")
            lb1 = sch.dma("sync", bcs[0:1, 0:512], bout[512:1024].rearrange("(o j) -> o j", o=1), stream="cst")
            m1 = vop(lambda e: e.tensor_copy(out=bvb[:], in_=bqkvT[:, 16:24]),
                     waits=[bk_mark])
            last = None
            for c in range(8):
                last = top(lambda e, c=c: e.matmul(
                    ps[0][0:1, :], bvb[:, c:c + 1], wo_sb[:, c * 1024:c * 1024 + 512],
                    start=(c == 0), stop=(c == 7)),
                    waits=[m1, mk_wo] if c == 0 else [])
            for c in range(8):
                last = top(lambda e, c=c: e.matmul(
                    ps[1][0:1, :], bvb[:, c:c + 1], wo_sb[:, c * 1024 + 512:(c + 1) * 1024],
                    start=(c == 0), stop=(c == 7)), waits=[])
            e1 = vop(lambda e: e.tensor_tensor(out=recf[0:1, :], in0=ps[0][0:1, :],
                                               in1=recf[0:1, :], op=ALU.add),
                     waits=[last, lb0, lb1])
            e2 = vop(lambda e: e.tensor_tensor(out=bcs[0:1, :], in0=ps[1][0:1, :],
                                               in1=bcs[0:1, :], op=ALU.add), waits=[])
            b1 = top(lambda e: e.matmul(ps[2][:], onesr_f[:], recf[0:1, :],
                                        start=True, stop=True), waits=[e1, e2])
            b2 = top(lambda e: e.matmul(ps[3][:], onesr_f[:], bcs[0:1, :],
                                        start=True, stop=True), waits=[])
            vop(lambda e: e.tensor_copy(out=beff[:, 0:512], in_=ps[2][:]), waits=[b1, b2])
            vop(lambda e: e.tensor_copy(out=beff[:, 512:1024], in_=ps[3][:]), waits=[])
            sch.barrier()

        ln1_marks = {}
        for wi in range(NW):
            for j in range(4):
                tt = wi * 4 + j
                xt = x_sb[:, tt * 1024:(tt + 1) * 1024]
                vop(lambda e, xt=xt, tt=tt: e.tensor_reduce(
                    out=st_sx[:, tt:tt + 1], in_=xt, axis=AX.X, op=ALU.add),
                    waits=[mk_xt[tt]])
                aop(lambda e, xt=xt, tt=tt: e.activation(
                    out=sqj[:], in_=xt, func=AF.Square, accum_out=st_sq[:, tt:tt + 1]),
                    waits=[mk_xt[tt]])
            ln1_marks[wi] = ln_chain(st_sx, st_sq, st_m, st_r, st_a, st_b,
                                     slice(wi * 4, wi * 4 + 4))

        x1_marks = {}
        vt_marks = {}
        kt_marks = {}
        ln2q_marks = {}
        state = {"tr_free": None, "hln_rd": [None]}
        bank_rd = [None] * 7
        slice_rd = [None, None, None, None]
        vslice_rd = [None, None]
        pt_free = [None, None]
        bcs_rd = None
        recb_rd = None

        def emit_hlnT(wi):
            last = None
            for j in range(4):
                tt = wi * 4 + j
                hb = hln[0]
                ap_m = vop(lambda e, tt=tt, hb=hb: e.tensor_scalar(
                    out=hb[:], in0=x_sb[:, tt * 1024:(tt + 1) * 1024],
                    scalar1=st_a[:, tt:tt + 1], scalar2=st_b[:, tt:tt + 1],
                    op0=ALU.mult, op1=ALU.add),
                    waits=[ln1_marks[wi], state["hln_rd"][0]])
                trs = None
                for c in range(8):
                    trs = top(lambda e, hb=hb, c=c: e.transpose(
                        tr_ps[:, c * 128:(c + 1) * 128], hb[:, c * 128:(c + 1) * 128],
                        ident_bf[:]), waits=[ap_m, state["tr_free"]] if c == 0 else [])
                state["hln_rd"][0] = trs
                last = vop(lambda e, j=j: e.tensor_copy(
                    out=hlnT[:].rearrange("p (c t) -> p c t", c=8)[:, :, j * 128:(j + 1) * 128],
                    in_=tr_ps[:].rearrange("p (c t) -> p c t", c=8)), waits=[trs])
                state["tr_free"] = last
            return last


        def emit_ln2_group(gi):
            for j in range(4):
                tt = gi * 4 + j
                xt = x_sb[:, tt * 1024:(tt + 1) * 1024]
                vop(lambda e, xt=xt, tt=tt: e.tensor_reduce(
                    out=st_sx[:, tt:tt + 1], in_=xt, axis=AX.X, op=ALU.add),
                    waits=[x1_marks[tt]])
                vop(lambda e, xt=xt, tt=tt: e.tensor_reduce(
                    out=st_mx[:, tt:tt + 1], in_=xt, axis=AX.X, op=ALU.max), waits=[])
                vop(lambda e, xt=xt, tt=tt: e.tensor_reduce(
                    out=st_mn[:, tt:tt + 1], in_=xt, axis=AX.X, op=ALU.min), waits=[])
                aop(lambda e, xt=xt, tt=tt: e.activation(
                    out=sqj_g[:], in_=xt, func=AF.Square, accum_out=st_sq[:, tt:tt + 1]),
                    waits=[x1_marks[tt]])
            sl = slice(gi * 4, gi * 4 + 4)
            ln2 = ln_chain(st_sx, st_sq, st_m, st_r, st_a, st_b, sl)
            vop(lambda e, sl=sl: e.tensor_tensor(out=st_mx[:, sl], in0=st_mx[:, sl],
                                          in1=st_m[:, sl], op=ALU.subtract), waits=[ln2])
            vop(lambda e, sl=sl: e.tensor_tensor(out=st_mn[:, sl], in0=st_m[:, sl],
                                          in1=st_mn[:, sl], op=ALU.subtract))
            vop(lambda e, sl=sl: e.tensor_tensor(out=st_mx[:, sl], in0=st_mx[:, sl],
                                          in1=st_mn[:, sl], op=ALU.max))
            vop(lambda e, sl=sl: e.tensor_tensor(out=st_mx[:, sl], in0=st_mx[:, sl],
                                          in1=st_r[:, sl], op=ALU.mult))
            mxc = vop(lambda e, sl=sl: e.tensor_single_scalar(out=st_mx[:, sl], in_=st_mx[:, sl],
                                                       scalar=1e-5, op=ALU.max))
            vop(lambda e, sl=sl: e.reciprocal(out=st_c1[:, sl], in_=st_mx[:, sl]), waits=[mxc])
            sxm = vop(lambda e, sl=sl: e.tensor_single_scalar(out=st_c1[:, sl], in_=st_c1[:, sl],
                                                       scalar=127.0, op=ALU.mult))
            vop(lambda e, sl=sl: e.tensor_tensor(out=st_a[:, sl], in0=st_r[:, sl],
                                          in1=st_c1[:, sl], op=ALU.mult), waits=[sxm])
            vop(lambda e, sl=sl: e.tensor_tensor(out=st_b[:, sl], in0=st_m[:, sl],
                                          in1=st_a[:, sl], op=ALU.mult))
            vop(lambda e, sl=sl: e.tensor_scalar(out=st_b[:, sl], in0=st_b[:, sl],
                                          scalar1=cst[:, 5:6], scalar2=cst[:, 0:1],
                                          op0=ALU.mult, op1=ALU.add), waits=[m_magic])
            vop(lambda e, sl=sl: e.tensor_single_scalar(out=st_mx[:, sl], in_=st_mx[:, sl],
                                                 scalar=1.0 / 127.0, op=ALU.mult))
            vop(lambda e, sl=sl: e.tensor_single_scalar(out=st_c1[:, sl], in_=st_mx[:, sl],
                                                 scalar=mcb[:, 0:1], op=ALU.mult))
            ln2q_marks[gi] = vop(lambda e, sl=sl: e.tensor_single_scalar(
                out=st_c2[:, sl], in_=st_mx[:, sl], scalar=mcb[:, 1:2], op=ALU.mult))

        hT_next = emit_hlnT(0)
        for w in range(NW):
            qs = w * 512
            hT_ready = hT_next

            q_marks = {}
            for f in range(16):
                buf = wsl[f % 4]
                src = wqkv_bf[:, f * 128:(f + 1) * 128].rearrange("(c p) j -> p c j", p=128)
                ldm = sch.dma("sync", buf[:].rearrange("p (c j) -> p c j", c=8), src,
                              waits=[slice_rd[f % 4]], stream=f"wsl{f % 4}")
                bank = ps[f % 2]
                mml = None
                for c in range(8):
                    mml = top(lambda e, bank=bank, buf=buf, c=c: e.matmul(
                        bank[:], buf[:, c * 128:(c + 1) * 128],
                        hlnT[:, c * 512:(c + 1) * 512],
                        start=(c == 0), stop=(c == 7)),
                        waits=[ldm, hT_ready, bank_rd[f % 2]] if c == 0 else [])
                slice_rd[f % 4] = mml
                if f < 8:
                    em = vop(lambda e, bank=bank, f=f: e.tensor_single_scalar(
                        out=QTw[:, f * 512:(f + 1) * 512], in_=bank[:],
                        scalar=bqkvT[:, f:f + 1], op=ALU.add), waits=[mml, bk_mark])
                    q_marks[f] = em
                else:
                    em = vop(lambda e, bank=bank, f=f, qs=qs: e.tensor_scalar(
                        out=KT[:, (f - 8) * S + qs:(f - 8) * S + qs + 512], in0=bank[:],
                        scalar1=cst[:, 1:2], scalar2=bqkvT[:, f:f + 1],
                        op0=ALU.mult, op1=ALU.add), waits=[mml, bk_mark])
                    kt_marks[(f - 8, w)] = em
                bank_rd[f % 2] = em

            for h in range(2):
                buf = wvs[h]
                src = wqkv_bf[:, 2048 + h * 512:2048 + (h + 1) * 512].rearrange(
                    "(c p) j -> p c j", p=128)
                ldm = sch.dma("sync", buf[:].rearrange("p (c j) -> p c j", c=8), src,
                              waits=[vslice_rd[0], vslice_rd[1]], stream="wvs")
                for j in range(4):
                    tt = w * 4 + j
                    bi = 2 + (j % 2)
                    bank = ps[bi]
                    mml = None
                    for c in range(8):
                        mml = top(lambda e, bank=bank, buf=buf, c=c, j=j: e.matmul(
                            bank[:], hlnT[:, c * 512 + j * 128:c * 512 + (j + 1) * 128],
                            buf[:, c * 512:(c + 1) * 512],
                            start=(c == 0), stop=(c == 7)),
                            waits=[ldm, hT_ready, bank_rd[bi]] if c == 0 else [])
                    em = aop(lambda e, bank=bank, tt=tt, h=h: e.activation(
                        out=VT[:, tt * 1024 + h * 512:tt * 1024 + (h + 1) * 512],
                        in_=bank[:], func=AF.Copy), waits=[mml])
                    vt_marks[(tt, h)] = em
                    bank_rd[bi] = em
                    vslice_rd[h] = mml

            if w + 1 < NW:
                hT_next = emit_hlnT(w + 1)
            if w > 0:
                emit_ln2_group(w - 1)

            if s_idx == 0 and w == 0:
                tap("d_QTw", QTw[:], waits=[q_marks[p_] for p_ in range(8)])
                tap("d_sq", st_sq[:], waits=[ln1_marks[NW - 1]])
                tap("d_sta", st_a[:], waits=[ln1_marks[NW - 1]])
            nkb = 4 * w + 4
            for p in range(8):
                exp_marks = {}
                pend = []
                vp = vps[p % 2]
                vbi = 4 + (p % 2)
                z1 = vop(lambda e, vp=vp: e.memset(vp[64:128, :], 0.0),
                         waits=[bank_rd[vbi]])
                z2 = vop(lambda e: e.memset(sg[32:33, :], 0.0),
                         waits=[bank_rd[6]])
                bank_rd[vbi] = z1
                bank_rd[6] = z2

                def emit_sv(kprev, offp):
                    pts = PT[kprev % 2]
                    st0 = (kprev == 0)
                    sp0_ = (kprev == nkb - 1)
                    top(lambda e, o=offp, b=pts, st=st0, sp=sp0_: e.matmul(
                        sg[0:1, o:512], ones_col[:], b[:, o:512],
                        start=st, stop=sp),
                        waits=[exp_marks[kprev]] +
                              ([bank_rd[6]] if kprev == 0 else []))
                    top(lambda e, o=offp, b=pts: e.matmul(
                        sg[32:33, o:512], ones_col[:], b[:, 512 + o:1024],
                        start=False, stop=False, skip_group_check=True), waits=[])
                    top(lambda e, o=offp, p=p, k=kprev, b=pts, st=st0, sp=sp0_, vp=vp: e.matmul(
                        vp[0:64, o:512],
                        VT[:, k * 1024 + 2 * p * 64:k * 1024 + (2 * p + 1) * 64],
                        b[:, o:512], start=st, stop=sp,
                        tile_position=(0, 0)),
                        waits=[vt_marks[(kprev, (2 * p * 64) // 512)],
                               vt_marks[(kprev, ((2 * p + 1) * 64) // 512)]] +
                              ([bank_rd[vbi]] if kprev == 0 else []))
                    vm_ = top(lambda e, o=offp, p=p, k=kprev, b=pts, vp=vp: e.matmul(
                        vp[64:128, o:512],
                        VT[:, k * 1024 + (2 * p + 1) * 64:k * 1024 + (2 * p + 2) * 64],
                        b[:, 512 + o:1024], start=False, stop=False,
                        tile_position=(0, 64), skip_group_check=True), waits=[])
                    pt_free[kprev % 2] = vm_
                    return vm_

                for kb in range(nkb):
                    off = max(0, kb * 128 - qs)
                    sb_t = sp[kb % 2]
                    straddle = kb * 128 >= qs
                    deps = [bank_rd[2 * (kb % 2)], bank_rd[2 * (kb % 2) + 1],
                            kt_marks[(p, kb // 4)], q_marks[p], hT_ready]
                    first = True
                    if straddle:
                        top(lambda e, b=sb_t, off=off: e.matmul(
                            b[:, off:off + 128], ident_bf[:], tri_bf[:],
                            start=True, stop=False), waits=deps)
                        top(lambda e, b=sb_t, off=off: e.matmul(
                            b[:, 512 + off:512 + off + 128], ident_bf[:], tri_bf[:],
                            start=True, stop=False), waits=[])
                        first = False
                        spans = [(off, off + 128, False)]
                        if off + 128 < 512:
                            spans.append((off + 128, 512, True))
                    else:
                        spans = [(off, 512, True)]
                    sc_m = None
                    for (o0, o1, st_flag) in spans:
                        top(lambda e, b=sb_t, o0=o0, o1=o1, p=p, kb=kb, st=st_flag: e.matmul(
                            b[:, o0:o1],
                            KT[0:64, p * S + kb * 128:p * S + (kb + 1) * 128],
                            QTw[0:64, p * 512 + o0:p * 512 + o1],
                            start=st, stop=True, tile_position=(0, 0)),
                            waits=deps if first else [])
                        first = False
                        sc_m = top(lambda e, b=sb_t, o0=o0, o1=o1, p=p, kb=kb, st=st_flag: e.matmul(
                            b[:, 512 + o0:512 + o1],
                            KT[64:128, p * S + kb * 128:p * S + (kb + 1) * 128],
                            QTw[64:128, p * 512 + o0:p * 512 + o1],
                            start=st, stop=True, tile_position=(64, 0)), waits=[])

                    if len(pend) == 2:
                        emit_sv(*pend.pop(0))
                    ew = [sc_m, pt_free[kb % 2]]
                    e1 = aop(lambda e, b=sb_t, off=off, o_=PT[kb % 2]: e.activation(
                        out=o_[:].rearrange("p (h q) -> p h q", h=2)[:, :, off:512],
                        in_=b[:].rearrange("p (h q) -> p h q", h=2)[:, :, off:512],
                        func=AF.Exp), waits=ew)
                    exp_marks[kb] = e1
                    bank_rd[2 * (kb % 2)] = e1
                    bank_rd[2 * (kb % 2) + 1] = e1
                    pend.append((kb, off))
                vals_mm = None
                while pend:
                    vals_mm = emit_sv(*pend.pop(0))

                l1 = aop(lambda e: e.activation(out=recf[0:1, :], in_=sg[0:1, :],
                                                func=AF.Ln), waits=[sch.snap("pe")])
                l2 = aop(lambda e: e.activation(out=recf[32:33, :], in_=sg[32:33, :],
                                                func=AF.Ln), waits=[])
                c1 = aop(lambda e: e.activation(out=recb[0:1, 0:512], in_=recf[0:1, :],
                                                func=AF.Exp, scale=-1.0),
                         waits=[recb_rd, l1])
                c2 = aop(lambda e: e.activation(out=recb[0:1, 512:1024], in_=recf[32:33, :],
                                                func=AF.Exp, scale=-1.0), waits=[l2])
                bm1 = top(lambda e: e.matmul(sg[0:64, :], onesr_bf[0:1, 0:64],
                                             recb[0:1, 0:512], start=True, stop=True,
                                             tile_position=(0, 0)), waits=[c1, c2])
                bm2 = top(lambda e: e.matmul(sg[64:128, :], onesr_bf[0:1, 0:64],
                                             recb[0:1, 512:1024], start=True, stop=True,
                                             tile_position=(0, 64)), waits=[])
                recb_rd = bm2
                be = aop(lambda e: e.activation(out=bcs[:], in_=sg[:], func=AF.Copy),
                         waits=[bm2, bcs_rd])
                bank_rd[6] = be
                vm = vop(lambda e, p=p, vp=vp: e.tensor_tensor(
                    out=valsT[:, p * 512:(p + 1) * 512], in0=vp[:], in1=bcs[:],
                    op=ALU.mult), waits=[vals_mm, be])
                bank_rd[vbi] = vm
                bcs_rd = vm
                if s_idx == 0 and w == 0 and p == 7:
                    tap("d_valsT", valsT[:], waits=[vm])
                    tap("d_recb", recb[:], waits=[vm])

            for jp in range(2):
                for jj in range(2):
                    j = jp * 2 + jj
                    tt = w * 4 + j
                    b0i = 0 + jj * 4
                    b1i = 1 + jj * 4
                    mml = None
                    for c in range(8):
                        mml = top(lambda e, c=c, j=j, b=ps[b0i]: e.matmul(
                            b[:], valsT[:, c * 512 + j * 128:c * 512 + (j + 1) * 128],
                            wo_sb[:, c * 1024:c * 1024 + 512],
                            start=(c == 0), stop=(c == 7)),
                            waits=[bank_rd[4], bank_rd[5], bank_rd[b0i], mk_wo] if c == 0 else [])
                    for c in range(8):
                        mml = top(lambda e, c=c, j=j, b=ps[b1i]: e.matmul(
                            b[:], valsT[:, c * 512 + j * 128:c * 512 + (j + 1) * 128],
                            wo_sb[:, c * 1024 + 512:(c + 1) * 1024],
                            start=(c == 0), stop=(c == 7)),
                            waits=[bank_rd[b1i]] if c == 0 else [])
                    ev0 = vop(lambda e, jp=jp, jj=jj, b=ps[b0i]: e.tensor_tensor(
                        out=attn[:, (jp * 2 + jj) * 1024:(jp * 2 + jj) * 1024 + 512], in0=b[:],
                        in1=beff[:, 0:512], op=ALU.add), waits=[mml])
                    ev1 = vop(lambda e, jp=jp, jj=jj, b=ps[b1i]: e.tensor_tensor(
                        out=attn[:, (jp * 2 + jj) * 1024 + 512:(jp * 2 + jj + 1) * 1024], in0=b[:],
                        in1=beff[:, 512:1024], op=ALU.add), waits=[])
                    bank_rd[b0i] = ev0
                    bank_rd[b1i] = ev1
                    vop(lambda e, jj=jj, jp=jp, w=w: e.tensor_reduce(
                        out=st_w[:, w * 4 + jp * 2 + jj:w * 4 + jp * 2 + jj + 1],
                        in_=attn[:, (jp * 2 + jj) * 1024:(jp * 2 + jj + 1) * 1024],
                        axis=AX.X, op=ALU.add), waits=[ev0, ev1])
                    aop(lambda e, jj=jj, jp=jp, w=w: e.activation(
                        out=sqj[:], in_=attn[:, (jp * 2 + jj) * 1024:(jp * 2 + jj + 1) * 1024],
                        func=AF.Square, accum_out=st_w2[:, w * 4 + jp * 2 + jj:w * 4 + jp * 2 + jj + 1]),
                        waits=[ev0, ev1])
            ln3 = ln_chain(st_w, st_w2, st_wm, st_wr, st_wa, st_wb,
                           slice(w * 4, w * 4 + 4))
            for j in range(4):
                tt = w * 4 + j
                t1 = vop(lambda e, j=j, w=w: e.tensor_scalar(
                    out=attn[:, j * 1024:(j + 1) * 1024],
                    in0=attn[:, j * 1024:(j + 1) * 1024],
                    scalar1=st_wa[:, w * 4 + j:w * 4 + j + 1],
                    scalar2=st_wb[:, w * 4 + j:w * 4 + j + 1],
                    op0=ALU.mult, op1=ALU.add), waits=[ln3])
                x1_marks[tt] = vop(lambda e, j=j, tt=tt: e.tensor_tensor(
                    out=x_sb[:, tt * 1024:(tt + 1) * 1024],
                    in0=attn[:, j * 1024:(j + 1) * 1024],
                    in1=x_sb[:, tt * 1024:(tt + 1) * 1024], op=ALU.add), waits=[t1])
            if w == NW - 1:
                emit_ln2_group(w)
            if s_idx == 0 and w == NW - 1:
                tap("d_KT", KT[:], waits=[kt_marks[(p_, NW - 1)] for p_ in range(8)])
                tap("d_VT", VT[:], waits=[vt_marks[(NT - 1, 0)], vt_marks[(NT - 1, 1)]])
                tap("d_x1", x_sb[:], waits=[x1_marks[tt_] for tt_ in range(NT)])
        pe_ab = sch.snap("pe")
        esA.close()

        esB = ExitStack()
        bb = lambda n, s_, d=F32: esB.enter_context(nc.sbuf_tensor(f"s{s_idx}b_{n}", s_, d))
        y_bf = bb("y_bf", [128, 4 * 4096], BF16)
        wbuf = bb("wbuf", [128, 16384], BF16)
        yqT = bb("yqT", [128, 16384], BF16)
        q1b = bb("q1b", [128, 1024], F32)
        q1c = bb("q1c", [128, 1024], F32)
        yq_t = bb("yq_t", [128, 1024], BF16)
        yq_c = bb("yq_c", [128, 1024], BF16)
        h2q = [bb("h2q0", [128, 1024], BF16)] * 2
        h2qT = bb("h2qT", [128, 8 * 512], BF16)
        silb0 = bb("silb0", [128, 512], F32)
        silb = [silb0, silb0]
        o_sb = bb("o_sb", [128, 4 * 1024], F32)
        sqj2 = sqj_g

        tr_free = pe_ab
        trB_free = pe_ab
        h2q_rd = [pe_ab, pe_ab]
        wbuf_rd = [pe_ab, pe_ab]
        bank_rd = [None] * 7
        sil_rd = [None]
        yq_rd = [pe_ab, pe_ab]
        ost_marks = {}
        def emit_h2q(g, gu_done):
            nonlocal tr_free
            last = None
            for j in range(4):
                tt = g * 4 + j
                hb = h2q[j % 2]
                a1 = vop(lambda e, tt=tt, g=g: e.tensor_scalar(
                    out=q1b[:], in0=x_sb[:, tt * 1024:(tt + 1) * 1024],
                    scalar1=st_a[:, tt:tt + 1], scalar2=st_b[:, tt:tt + 1],
                    op0=ALU.mult, op1=ALU.add), waits=[h2q_rd[j % 2], ln2q_marks[g]])
                a2m = vop(lambda e, hb=hb: e.tensor_single_scalar(
                    out=hb[:], in_=q1b[:], scalar=cst[:, 0:1],
                    op=ALU.subtract), waits=[a1])
                trs = None
                for c in range(8):
                    trs = top(lambda e, hb=hb, c=c: e.transpose(
                        tr_ps[:, c * 128:(c + 1) * 128], hb[:, c * 128:(c + 1) * 128],
                        ident_bf[:]), waits=[a2m, tr_free] if c == 0 else [])
                h2q_rd[j % 2] = trs
                last = vop(lambda e, j=j: e.tensor_copy(
                    out=h2qT[:].rearrange("p (c t) -> p c t", c=8)[:, :, j * 128:(j + 1) * 128],
                    in_=tr_ps[:].rearrange("p (c t) -> p c t", c=8)),
                    waits=[trs] + gu_done)
                tr_free = last
            return last

        hT_pend = None
        for g in range(4):
            if hT_pend is None:
                h2qT_ready = emit_h2q(g, [])
            else:
                h2qT_ready = hT_pend

            y_marks = []
            red_marks = {}
            for iw in range(8):
                reg = iw % 2
                gsl = wbuf[:, reg * 8192:reg * 8192 + 4096]
                usl = wbuf[:, reg * 8192 + 4096:reg * 8192 + 8192]
                lg = sch.dma("sync", gsl.rearrange("p (c j) -> p c j", c=8),
                             wgq[:, iw * 512:(iw + 1) * 512].rearrange(
                                 "(c p) j -> p c j", p=128), waits=[wbuf_rd[reg]],
                             stream=f"gu{reg}")
                lu = sch.dma("sync", usl.rearrange("p (c j) -> p c j", c=8),
                             wuq[:, iw * 512:(iw + 1) * 512].rearrange(
                                 "(c p) j -> p c j", p=128), waits=[],
                             stream=f"gu{reg}")
                for j in range(4):
                    tt = g * 4 + j
                    par = j % 2
                    bg, bu = pb[par * 2], pb[par * 2 + 1]
                    sb_ = silb[par]
                    mml = None
                    for c in range(8):
                        mml = top(lambda e, bg=bg, gsl=gsl, c=c, j=j: e.matmul(
                            bg[:], h2qT[:, c * 512 + j * 128:c * 512 + (j + 1) * 128],
                            gsl[:, c * 512:(c + 1) * 512],
                            start=(c == 0), stop=(c == 7)),
                            waits=[lg, lu, h2qT_ready, bank_rd[par * 2]] if c == 0 else [])
                    for c in range(8):
                        mml = top(lambda e, bu=bu, usl=usl, c=c, j=j: e.matmul(
                            bu[:], h2qT[:, c * 512 + j * 128:c * 512 + (j + 1) * 128],
                            usl[:, c * 512:(c + 1) * 512],
                            start=(c == 0), stop=(c == 7)),
                            waits=[bank_rd[par * 2 + 1]] if c == 0 else [])
                    wbuf_rd[reg] = mml
                    sm = aop(lambda e, bg=bg, tt=tt, sb_=sb_: e.activation(
                        out=sb_[:], in_=bg[:], func=AF.Silu,
                        scale=st_c1[:, tt:tt + 1]), waits=[mml, sil_rd[0], ln2q_marks[g]])
                    ym = vop(lambda e, bu=bu, tt=tt, j=j, iw=iw, sb_=sb_: e.scalar_tensor_tensor(
                        out=y_bf[:, j * 4096 + iw * 512:j * 4096 + (iw + 1) * 512],
                        in0=bu[:], scalar=st_c2[:, tt:tt + 1], in1=sb_[:],
                        op0=ALU.mult, op1=ALU.mult), waits=[sm])
                    rm = vop(lambda e, j=j, iw=iw: e.tensor_reduce(
                        out=st_p[:, j * 8 + iw:j * 8 + iw + 1],
                        in_=y_bf[:, j * 4096 + iw * 512:j * 4096 + (iw + 1) * 512],
                        axis=AX.X, op=ALU.max, apply_absolute_value=True), waits=[ym])
                    red_marks[(j, iw)] = rm
                    sil_rd[0] = ym
                    bank_rd[par * 2] = sm
                    bank_rd[par * 2 + 1] = ym
                    y_marks.append(ym)
            y_done = y_marks[-1]
            if s_idx == 0 and g == 0:
                tap("d_h2qT", h2qT[:], waits=[h2qT_ready])
                tap("d_y", y_bf[:], waits=[y_done])

            cd_marks = {}
            sy_marks = {}
            for j in range(4):
                tt = g * 4 + j
                c0m = vop(lambda e, j=j, tt=tt: e.tensor_reduce(
                    out=st_my[:, tt:tt + 1], in_=st_p[:, j * 8:(j + 1) * 8],
                    axis=AX.X, op=ALU.max), waits=[red_marks[(j, 7)]])
                c1m = vop(lambda e, tt=tt: e.tensor_single_scalar(
                    out=st_my[:, tt:tt + 1], in_=st_my[:, tt:tt + 1],
                    scalar=1e-5, op=ALU.max))
                c2m = vop(lambda e, tt=tt: e.reciprocal(
                    out=st_cd[:, tt:tt + 1], in_=st_my[:, tt:tt + 1]), waits=[c1m])
                sy_marks[j] = vop(lambda e, tt=tt: e.tensor_single_scalar(
                    out=st_cd[:, tt:tt + 1], in_=st_cd[:, tt:tt + 1],
                    scalar=127.0, op=ALU.mult), waits=[c2m])
                vop(lambda e, tt=tt: e.tensor_single_scalar(
                    out=st_my[:, tt:tt + 1], in_=st_my[:, tt:tt + 1],
                    scalar=1.0 / 127.0, op=ALU.mult), waits=[sy_marks[j]])
                cd_marks[j] = vop(lambda e, tt=tt: e.tensor_single_scalar(
                    out=st_my[:, tt:tt + 1], in_=st_my[:, tt:tt + 1],
                    scalar=mcb[:, 2:3], op=ALU.mult))
            cdm = cd_marks[3]

            def gop(fn, waits=(), inc="gp"):
                return sch.op("gpsimd", fn, waits=waits, inc=inc)

            tq_marks = {}
            dn_ev = {}
            wd_rd = None
            lw0 = sch.dma("sync", wbuf[:].rearrange("p (c j) -> p c j", c=32),
                          wdq[:, 0:512].rearrange("(c p) j -> p c j", p=128),
                          waits=[wbuf_rd[0], wbuf_rd[1]], stream="wd")

            trB = sp[0][:].bitcast(BF16)

            def emit_rt(qt):
                nonlocal yq_rd, tr_free, trB_free
                r1 = {}
                rl = {}
                tr = {}
                cp = {}

                def mk_r1(j, extra):
                    tt = g * 4 + j
                    qb = q1b if (qt + j) % 2 == 0 else q1c
                    return aop(lambda e, j=j, qt=qt, tt=tt, qb=qb: e.activation(
                        out=qb[:],
                        in_=y_bf[:, j * 4096 + qt * 1024:j * 4096 + (qt + 1) * 1024],
                        func=AF.Copy, scale=st_cd[:, tt:tt + 1]),
                        waits=[sy_marks[j], extra])

                def mk_rl(j, extra):
                    qb = q1b if (qt + j) % 2 == 0 else q1c
                    yqt = yq_t if (qt + j) % 2 == 0 else yq_c
                    return vop(lambda e, qb=qb, yqt=yqt: e.tensor_scalar(
                        out=yqt[:], in0=qb[:],
                        scalar1=cst[:, 0:1], scalar2=cst[:, 0:1],
                        op0=ALU.add, op1=ALU.subtract),
                        waits=[r1[j], extra])

                def mk_tr(j):
                    yqt = yq_t if (qt + j) % 2 == 0 else yq_c
                    tgt = tr_ps if j % 2 == 0 else trB
                    free = tr_free if j % 2 == 0 else trB_free
                    trs = None
                    for c in range(8):
                        trs = top(lambda e, c=c, yqt=yqt, tgt=tgt: e.transpose(
                            tgt[:, c * 128:(c + 1) * 128],
                            yqt[:, c * 128:(c + 1) * 128], ident_bf[:]),
                            waits=[rl[j], free, bank_rd[0] if j % 2 else None]
                            if c == 0 else [])
                    return trs

                def mk_cp(j):
                    tgt = tr_ps if j % 2 == 0 else trB
                    return vop(lambda e, j=j, qt=qt, tgt=tgt: e.tensor_copy(
                        out=yqT[:].rearrange("p (c t) -> p c t", c=32)
                            [:, qt * 8:(qt + 1) * 8, j * 128:(j + 1) * 128],
                        in_=tgt[:, 0:1024].rearrange("p (c t) -> p c t", c=8)),
                        waits=[tr[j]])

                r1[0] = mk_r1(0, yq_rd[qt % 2])
                r1[1] = mk_r1(1, yq_rd[(qt + 1) % 2])
                rl[0] = mk_rl(0, yq_rd[qt % 2])
                rl[1] = mk_rl(1, yq_rd[(qt + 1) % 2])
                tr[0] = mk_tr(0)
                yq_rd[qt % 2] = tr[0]
                r1[2] = mk_r1(2, rl[0])
                rl[2] = mk_rl(2, tr[0])
                cp[0] = mk_cp(0)
                tr_free = cp[0]
                tr[1] = mk_tr(1)
                yq_rd[(qt + 1) % 2] = tr[1]
                r1[3] = mk_r1(3, rl[1])
                rl[3] = mk_rl(3, tr[1])
                cp[1] = mk_cp(1)
                trB_free = cp[1]
                tr[2] = mk_tr(2)
                yq_rd[qt % 2] = tr[2]
                cp[2] = mk_cp(2)
                tr_free = cp[2]
                tr[3] = mk_tr(3)
                yq_rd[(qt + 1) % 2] = tr[3]
                cp[3] = mk_cp(3)
                trB_free = cp[3]
                bank_rd[0] = cp[3]
                return cp[3]

            def emit_down_chunk(hw, tp, ic0, ic1, lw):
                mml = None
                for ic in range(ic0, ic1):
                    for t2 in range(2):
                        j = tp * 2 + t2
                        bank = pb[2 + t2]
                        mmw = []
                        if ic == 0:
                            mmw = [lw, tq_marks[0], bank_rd[2 + t2],
                                   dn_ev.get((hw - 1, j)), dn_ev.get((hw, j - 2))]
                        elif ic % 8 == 0:
                            mmw = [tq_marks[ic // 8]]
                        mml = top(lambda e, ic=ic, j=j, bank=bank: e.matmul(
                            bank[:], yqT[:, ic * 512 + j * 128:ic * 512 + (j + 1) * 128],
                            wbuf[:, ic * 512:(ic + 1) * 512],
                            start=(ic == 0), stop=(ic == 31)), waits=mmw)
                return mml

            def emit_down_evac(hw, tp):
                pe_done = sch.snap("pe")
                for t2 in range(2):
                    j = tp * 2 + t2
                    tt = g * 4 + j
                    dn_ev[(hw, j)] = vop(lambda e, j=j, hw=hw, tt=tt, t2=t2: e.tensor_single_scalar(
                        out=o_sb[:, j * 1024 + hw * 512:j * 1024 + (hw + 1) * 512],
                        in_=pb[2 + t2][:], scalar=st_my[:, tt:tt + 1], op=ALU.mult),
                        waits=[pe_done, cd_marks[j], ost_marks.get(j)])
                    bank_rd[2 + t2] = dn_ev[(hw, j)]

            for qt in range(4):
                tq_marks[qt] = emit_rt(qt)
                wd_rd = emit_down_chunk(0, 0, qt * 8, (qt + 1) * 8, lw0)
            emit_down_evac(0, 0)
            wd_rd = emit_down_chunk(0, 1, 0, 32, lw0)
            emit_down_evac(0, 1)
            lw1 = sch.dma("sync", wbuf[:].rearrange("p (c j) -> p c j", c=32),
                          wdq[:, 512:1024].rearrange("(c p) j -> p c j", p=128),
                          waits=[wd_rd], stream="wd")
            wd_rd = emit_down_chunk(1, 0, 0, 32, lw1)
            if g + 1 < 4:
                hT_pend = emit_h2q(g + 1, [wbuf_rd[0], wbuf_rd[1]])
            emit_down_evac(1, 0)
            wd_rd = emit_down_chunk(1, 1, 0, 32, lw1)
            emit_down_evac(1, 1)
            wbuf_rd = [wd_rd, wd_rd]

            for j in range(4):
                tt = g * 4 + j
                w0 = [dn_ev[(0, j)], dn_ev[(1, j)]]
                vop(lambda e, tt=tt, j=j: e.tensor_reduce(
                    out=st_w[:, tt:tt + 1], in_=o_sb[:, j * 1024:(j + 1) * 1024],
                    axis=AX.X, op=ALU.add), waits=w0)
                aop(lambda e, tt=tt, j=j: e.activation(
                    out=sqj2[:], in_=o_sb[:, j * 1024:(j + 1) * 1024], func=AF.Square,
                    accum_out=st_w2[:, tt:tt + 1]), waits=w0)
            ln4 = ln_chain(st_w, st_w2, st_wm, st_wr, st_wa, st_wb,
                           slice(g * 4, g * 4 + 4))
            for j in range(4):
                tt = g * 4 + j
                t1 = vop(lambda e, tt=tt, j=j: e.tensor_scalar(
                    out=o_sb[:, j * 1024:(j + 1) * 1024],
                    in0=o_sb[:, j * 1024:(j + 1) * 1024],
                    scalar1=st_wa[:, tt:tt + 1], scalar2=st_wb[:, tt:tt + 1],
                    op0=ALU.mult, op1=ALU.add), waits=[ln4])
                om = vop(lambda e, j=j, tt=tt: e.tensor_tensor(
                    out=o_sb[:, j * 1024:(j + 1) * 1024],
                    in0=o_sb[:, j * 1024:(j + 1) * 1024],
                    in1=x_sb[:, tt * 1024:(tt + 1) * 1024], op=ALU.add), waits=[t1])
                ost_marks[j] = sch.dma("gpsimd", out_hbm[s_idx, tt * 128:(tt + 1) * 128, :],
                        o_sb[:, j * 1024:(j + 1) * 1024], waits=[om], stream=f"out{j}")
                if s_idx + 1 < NSEQ:
                    next_xmarks[tt] = sch.dma(
                        "gpsimd", x_sb[:, tt * 1024:(tt + 1) * 1024],
                        x_in[s_idx + 1, tt * 128:(tt + 1) * 128, :],
                        waits=[om], stream=f"xn{tt}")
                if s_idx == 0 and g == 0 and j == 3:
                    tap("d_osb", o_sb[:], waits=[om])
        sch.barrier()
        esB.close()
        return next_xmarks

    nxt = None
    for s_idx in range(NSEQ):
        nxt = emit_seq(s_idx, nxt)
    sch.barrier()

    with nc.Block() as block:
        sch.build(block)
    return nc


def kernel(L0_attn_x, L0_attn_W_qkv, L0_attn_b_qkv, L0_attn_W_out, L0_attn_b_out,
           L0_ffn_w_gate, L0_ffn_w_up, L0_ffn_w_down):
    x = np.ascontiguousarray(np.asarray(L0_attn_x, dtype=np.float32))
    in_common = {
        "wqkv": np.ascontiguousarray(np.asarray(L0_attn_W_qkv, np.float32)),
        "bqkv": np.ascontiguousarray(np.asarray(L0_attn_b_qkv, np.float32)),
        "wout": np.ascontiguousarray(np.asarray(L0_attn_W_out, np.float32)),
        "bout": np.ascontiguousarray(np.asarray(L0_attn_b_out, np.float32)),
        "wgT": np.ascontiguousarray(np.asarray(L0_ffn_w_gate, np.float32).T),
        "wuT": np.ascontiguousarray(np.asarray(L0_ffn_w_up, np.float32).T),
        "wdT": np.ascontiguousarray(np.asarray(L0_ffn_w_down, np.float32).T),
    }
    nc = build_program()
    in_maps = []
    for c in range(NCORES):
        m = dict(in_common)
        m["x"] = np.ascontiguousarray(x[c * NSEQ:(c + 1) * NSEQ])
        in_maps.append(m)
    res = run_bass_kernel_spmd(nc, in_maps, core_ids=list(range(NCORES)))
    return np.concatenate([r["out"] for r in res.results], axis=0)
